# version 33
# baseline (speedup 1.0000x reference)
import numpy as np
import ml_dtypes
from contextlib import ExitStack

import concourse.bass as bass
import concourse.mybir as mybir
from concourse.bass_utils import run_bass_kernel_spmd

F32 = mybir.dt.float32
BF16 = mybir.dt.bfloat16
AF = mybir.ActivationFunctionType
ALU = mybir.AluOpType
AX = mybir.AxisListType

NEG = -1.0e9
LN_EPS = 1e-5
D = 1024
DC = D // 128

ENGINES = ("tensor", "vector", "scalar", "sync", "gpsimd")
OWN_SEM = {"tensor": "st", "vector": "sv", "scalar": "ss", "gpsimd": "sg"}


class Res:
    __slots__ = ("w", "r")

    def __init__(self):
        self.w = None
        self.r = []


class Ev:
    __slots__ = ("sem", "seq", "used", "val")

    def __init__(self, sem, seq):
        self.sem = sem
        self.seq = seq
        self.used = False
        self.val = None


class Emit:

    def __init__(self, nc):
        self.nc = nc
        self.prog = []
        self.seqs = {}
        self.sems = {}
        self.ctx = ExitStack()

    def sem(self, name):
        if name not in self.sems:
            self.sems[name] = self.nc.semaphore(name).__enter__()
            self.seqs[name] = 0

    def op(self, eng, fn, reads=(), writes=(), sem=None, delta=1):
        if sem is None:
            sem = OWN_SEM[eng]
        self.sem(sem)
        waits = {}

        def add(ev):
            if ev is None:
                return
            k = ev.sem
            if k not in waits or waits[k].seq < ev.seq:
                waits[k] = ev

        for r in reads:
            add(r.w)
        for w in writes:
            add(w.w)
            for ev in w.r:
                add(ev)
        self.seqs[sem] += 1
        ev = Ev(sem, self.seqs[sem])
        if delta != 1:
            ev.used = True
        self.prog.append((eng, fn, list(waits.values()), ev, delta))
        for w in writes:
            w.w = ev
            w.r = []
        for r in reads:
            r.r.append(ev)
        return ev

    def raw(self, eng, fn):
        self.prog.append((eng, fn, [], None, 0))

    def flush(self):
        self.prog.append(("BARRIER", None, [], None, 0))

    def wait_event(self, eng, ev):
        self.prog.append((eng, None, [ev], None, 0))

    def finalize(self):
        for (_e, _f, waits, _ev, _d) in self.prog:
            for ev in waits:
                ev.used = True
        cum = {k: 0 for k in self.seqs}
        for (_e, _f, _w, ev, delta) in self.prog:
            if ev is not None and ev.used:
                cum[ev.sem] += delta
                ev.val = cum[ev.sem]
        sems = self.sems
        blocks = []
        cur = []
        for entry in self.prog:
            if entry[0] == "BARRIER":
                blocks.append(cur)
                cur = []
            else:
                cur.append(entry)
        if cur:
            blocks.append(cur)
        hw = {}
        for blk_ops in blocks:
            per_eng = {}
            for (eng, fn, waits, ev, delta) in blk_ops:
                final = []
                h = hw.setdefault(eng, {})
                for wv in waits:
                    assert wv.val is not None
                    if h.get(wv.sem, 0) < wv.val:
                        final.append((wv.sem, wv.val))
                        h[wv.sem] = wv.val
                inc = (ev.sem, ev.val, delta) if (
                    ev is not None and ev.used) else None
                per_eng.setdefault(eng, []).append((fn, final, inc))
            with self.nc.Block() as blk:
                for eng, ops in per_eng.items():
                    def body(E, ops=tuple(ops)):
                        for (fn, final, inc) in ops:
                            for (sname, v) in final:
                                E.wait_ge(sems[sname], v)
                            if fn is None:
                                continue
                            ins = fn(E)
                            if inc is not None:
                                ins.then_inc(sems[inc[0]], inc[2])
                    getattr(blk, eng)(body)


def build_nc(S=2048, NH=16, IDIM=2048, E=8, CAP=384, debug_stop=None):
    FC = NH // 2
    NB = S // 128
    NTC = S // 512
    IC = IDIM // 128
    IH = IDIM // 2
    NSC = CAP // 128
    assert S % 512 == 0 and NH % 2 == 0 and IDIM % 256 == 0

    nc = bass.Bass("TRN2", target_bir_lowering=False)

    def din(name, shape, dtype=F32):
        return nc.dram_tensor(name, shape, dtype, kind="ExternalInput")

    x_h = din("x", [D, S])
    c_h = din("cvec", [128, DC])
    wmod_h = din("wmod", [D, 6 * D])
    wq2_h = din("wq2", [FC, DC, 128, 128])
    wk2_h = din("wk2", [FC, DC, 128, 128])
    wv2_h = din("wv2", [FC, DC, 128, 128])
    wo_h = din("wo", [FC * DC, 128, 128])
    gw_h = din("gw", [DC, 128, E])
    shg_h = din("shg", [2, DC, 128, IH])
    shu_h = din("shu", [2, DC, 128, IH])
    shd_h = din("shd", [IDIM, D])
    rtg_h = din("rtg", [E, 2, DC, 128, IH])
    rtu_h = din("rtu", [E, 2, DC, 128, IH])
    rtd_h = din("rtd", [E, IDIM, D])
    diagm_h = din("diagm", [128, 128])
    ones128_h = din("ones128", [1, 128])
    nones128_h = din("nones128", [1, 128])
    onescol_h = din("onescol", [128, 1], BF16)
    eab_h = din("eab", [1, 256], BF16)
    ones128b_h = din("ones128b", [1, 128], BF16)
    identb_h = din("identb", [128, 128], BF16)
    identf_h = din("identf", [128, 128])
    w8_h = din("w8", [128, E])
    ufull_h = din("ufull", [128, 128], BF16)
    ustrict_h = din("ustrict", [128, 128], BF16)
    iotaf_h = din("iotaf", [128, 512])
    iotasc_h = din("iotasc", [128, CAP // 128])
    identh_h = din("identh", [128, 128], mybir.dt.float16)
    onesh_h = din("onesh", [1, 128], mybir.dt.float16)
    e65_h = din("e65", [68, 256], BF16)
    y_h = nc.dram_tensor("y", [D, S], F32, kind="ExternalOutput")

    em = Emit(nc)

    with em.ctx:
        ctx = em.ctx

        def sbuf(name, shape, dtype=F32, stack=None):
            return (stack or ctx).enter_context(
                nc.sbuf_tensor(name, shape, dtype))

        def psum(name, shape, stack=None):
            return (stack or ctx).enter_context(
                nc.psum_tensor(name, shape, F32))

        def dma(dst_ap, src_ap, writes, reads=(), eng="sync", sem=None):
            assert sem is not None
            return em.op(eng,
                         lambda E, d=dst_ap, s=src_ap: E.dma_start(d, s),
                         reads=reads, writes=writes, sem=sem, delta=16)

        def mmg(chain, writes, reads):
            n = len(chain)
            for i, (o, l, r) in enumerate(chain):
                em.op("tensor",
                      lambda E, o=o, l=l, r=r, a=(i == 0), z=(i == n - 1):
                      E.matmul(o, l, r, start=a, stop=z,
                               skip_group_check=True),
                      reads=(reads if (i == 0 or i == n - 1) else ()),
                      writes=(writes if (i == 0 or i == n - 1) else ()))

        Xf = [sbuf(f"Xf{i}", [128, S]) for i in range(DC)]
        Hb = [sbuf(f"Hb{i}", [128, S], BF16) for i in range(DC)]
        rXf = [Res() for _ in range(DC)]
        rHb = [Res() for _ in range(DC)]
        mod_sb = sbuf("mod_sb", [128, 6 * DC]); rmod = Res()
        c_sb = sbuf("c_sb", [128, DC]); rc = Res()
        c_sil = sbuf("c_sil", [128, DC], F32); rcs = Res()
        diagm = sbuf("diagm_s", [128, 128])
        ones128 = sbuf("ones128_s", [1, 128])
        nones128 = sbuf("nones128_s", [1, 128])
        onescol = sbuf("onescol_s", [128, 1], BF16)
        eab = sbuf("eab_s", [1, 256], BF16)
        ones128b = sbuf("ones128b_s", [1, 128], BF16)
        identb = sbuf("identb_s", [128, 128], BF16)
        identf = sbuf("identf_s", [128, 128])
        w8t = sbuf("w8t", [128, E])
        ufull = sbuf("ufull_s", [128, 128], BF16)
        ustrict = sbuf("ustrict_s", [128, 128], BF16)
        iotaf = sbuf("iotaf_s", [128, 512])
        iotasc = sbuf("iotasc_s", [128, CAP // 128])
        identh = sbuf("identh_s", [128, 128], mybir.dt.float16)
        onesh = sbuf("onesh_s", [1, 128], mybir.dt.float16)
        e65 = sbuf("e65_s", [68, 256], BF16)
        rconst = Res()
        sgn_t = sbuf("sgn_t", [128, DC]); rsgn = Res()
        oh_bf = sbuf("oh_bf", [128, NB * E], BF16); rohb = Res()
        oh_f = sbuf("oh_f", [128, NB * E]); rohf = Res()
        score_f = sbuf("score_f", [128, NB]); rscf = Res()
        rank_f = sbuf("rank_f", [128, NB * E]); rrk = Res()

        for i in range(DC):
            _last_sx = dma(Xf[i][:], x_h.ap()[i * 128:(i + 1) * 128, :],
                           writes=[rXf[i]], sem="sx")
        _last_sx = dma(c_sb[:], c_h.ap(), writes=[rc], sem="sx")
        for (t, h) in ((diagm, diagm_h), (ones128, ones128_h),
                       (nones128, nones128_h), (onescol, onescol_h),
                       (eab, eab_h), (ones128b, ones128b_h),
                       (identb, identb_h), (identf, identf_h), (w8t, w8_h),
                       (ufull, ufull_h), (ustrict, ustrict_h),
                       (iotaf, iotaf_h), (iotasc, iotasc_h),
                       (identh, identh_h), (onesh, onesh_h),
                       (e65, e65_h)):
            _last_sx = dma(t[:], h.ap(), writes=[rconst], sem="sx")
        for _r in [rc, rconst] + rXf:
            _r.w = _last_sx

        em.op("scalar", lambda E: E.activation(c_sil[:], c_sb[:], AF.Silu),
              reads=[rc], writes=[rcs])

        CA, GA, SB_, CB, GB = DC, 2 * DC, 3 * DC, 4 * DC, 5 * DC

        with ExitStack() as ph0:
            wmod_bf = [sbuf(f"wmodb{i}", [128, 6 * D], F32, ph0)
                       for i in range(2)]
            rwm = [Res(), Res()]
            mod_ps = psum("mod_ps", [128, 6 * DC], ph0)
            rmodps = Res()
            for dc in range(DC):
                b = dc % 2
                dma(wmod_bf[b][:], wmod_h.ap()[dc * 128:(dc + 1) * 128, :],
                    writes=[rwm[b]], eng="gpsimd", sem=f"dwm{b}")
                NJ = 6 * DC
                for j in range(NJ):
                    edge = (j == 0 or j == NJ - 1)
                    em.op("tensor",
                          lambda E, b=b, j=j, dc=dc: E.matmul(
                              mod_ps[:, j:j + 1],
                              wmod_bf[b][:, j * 128:(j + 1) * 128],
                              c_sil[:, dc:dc + 1],
                              start=(dc == 0 and j == 0),
                              stop=(dc == DC - 1 and j == NJ - 1),
                              skip_group_check=True),
                          reads=([rwm[b], rcs] if edge else ()),
                          writes=([rmodps] if edge and
                                  (dc == 0 or dc == DC - 1) else ()))
            em.op("scalar", lambda E: E.copy(mod_sb[:], mod_ps[:]),
                  reads=[rmodps], writes=[rmod])

            em.op("vector",
                  lambda E: E.tensor_scalar_add(mod_sb[:, CA:CA + DC],
                                                mod_sb[:, CA:CA + DC], 1.0),
                  reads=[rmod], writes=[rmod])
            em.op("vector",
                  lambda E: E.tensor_scalar_add(mod_sb[:, CB:CB + DC],
                                                mod_sb[:, CB:CB + DC], 1.0),
                  reads=[rmod], writes=[rmod])
            for G in (GA, GB):
                em.op("vector",
                      lambda E, G=G: E.tensor_scalar(
                          sgn_t[:], mod_sb[:, G:G + DC], 0.0, 2.0,
                          ALU.is_ge, ALU.mult),
                      reads=[rmod], writes=[rsgn])
                em.op("vector",
                      lambda E: E.tensor_scalar_add(sgn_t[:], sgn_t[:], -1.0),
                      reads=[rsgn], writes=[rsgn])
                em.op("scalar",
                      lambda E, G=G: E.activation(
                          mod_sb[:, G:G + DC], mod_sb[:, G:G + DC], AF.Abs),
                      reads=[rmod], writes=[rmod])
                em.op("vector",
                      lambda E, G=G: E.tensor_scalar_max(
                          mod_sb[:, G:G + DC], mod_sb[:, G:G + DC], 0.25),
                      reads=[rmod], writes=[rmod])
                em.op("vector",
                      lambda E, G=G: E.tensor_mul(
                          mod_sb[:, G:G + DC], mod_sb[:, G:G + DC],
                          sgn_t[:]),
                      reads=[rmod, rsgn], writes=[rmod])
            em.flush()

        def emit_ln(cG, sG, tag, post_tc=None, do_flush=True):
            with ExitStack() as lnx:
                rstd_row = sbuf(f"rstd_row{tag}", [1, S], F32, lnx)
                rrstd = Res()
                negm_row = sbuf(f"negm_row{tag}", [1, S], F32, lnx)
                rnegm = Res()
                sq_buf = [sbuf(f"sqb{tag}{i}", [128, 512], BF16, lnx)
                          for i in range(2)]
                rsq = [Res(), Res()]
                app_t = sbuf(f"app_t{tag}", [128, 512], F32, lnx)
                rapp = Res()
                stat_ps = psum(f"stat_ps{tag}", [1, 512], lnx); r1 = Res()
                stat2_ps = psum(f"stat2_ps{tag}", [1, 512], lnx); r2 = Res()
                bcA_ps = psum(f"bcA_ps{tag}", [128, 512], lnx); rbA = Res()
                bcB_ps = psum(f"bcB_ps{tag}", [128, 512], lnx); rbB = Res()
                for dc in range(DC):
                    em.op("vector",
                          lambda E, d=dc: E.tensor_copy(Hb[d][:], Xf[d][:]),
                          reads=[rXf[dc]], writes=[rHb[dc]])
                for tc in range(NTC):
                    sl = slice(tc * 512, (tc + 1) * 512)
                    for dc in range(DC):
                        em.op("scalar",
                              lambda E, d=dc, s=sl: E.activation(
                                  sq_buf[d % 2][:], Hb[d][:, s], AF.Square),
                              reads=[rHb[dc]], writes=[rsq[dc % 2]])
                        em.op("tensor",
                              lambda E, d=dc, s=sl: E.matmul(
                                  stat_ps[:], onescol[:], Hb[d][:, s],
                                  start=(d == 0), stop=(d == DC - 1),
                                  skip_group_check=True),
                              reads=[rHb[dc], rconst],
                              writes=([r1] if dc in (0, DC - 1) else ()))
                        em.op("tensor",
                              lambda E, d=dc: E.matmul(
                                  stat2_ps[:], onescol[:], sq_buf[d % 2][:],
                                  start=(d == 0), stop=(d == DC - 1),
                                  skip_group_check=True),
                              reads=[rsq[dc % 2]],
                              writes=([r2] if dc in (0, DC - 1) else ()))
                    em.op("scalar",
                          lambda E, s=sl: E.copy(negm_row[:, s], stat_ps[:]),
                          reads=[r1], writes=[rnegm])
                    em.op("scalar",
                          lambda E, s=sl: E.copy(rstd_row[:, s], stat2_ps[:]),
                          reads=[r2], writes=[rrstd])

                    em.op("vector",
                          lambda E, s=sl: E.tensor_scalar_mul(
                              negm_row[:, s], negm_row[:, s], 1.0 / D),
                          reads=[rnegm], writes=[rnegm])
                    em.op("vector",
                          lambda E, s=sl: E.tensor_scalar_mul(
                              rstd_row[:, s], rstd_row[:, s], 1.0 / D),
                          reads=[rrstd], writes=[rrstd])
                    em.op("vector",
                          lambda E, s=sl: E.tensor_mul(
                              stat_ps[:], negm_row[:, s], negm_row[:, s]),
                          reads=[rnegm], writes=[r1])
                    em.op("vector",
                          lambda E, s=sl: E.tensor_sub(
                              rstd_row[:, s], rstd_row[:, s], stat_ps[:]),
                          reads=[rrstd, r1], writes=[rrstd])
                em.op("vector",
                      lambda E: E.tensor_scalar_add(rstd_row[:], rstd_row[:],
                                                    LN_EPS),
                      reads=[rrstd], writes=[rrstd])
                em.op("scalar",
                      lambda E: E.activation(rstd_row[:], rstd_row[:], AF.Ln),
                      reads=[rrstd], writes=[rrstd])
                em.op("vector",
                      lambda E: E.tensor_scalar_mul(rstd_row[:], rstd_row[:],
                                                    -0.5),
                      reads=[rrstd], writes=[rrstd])
                em.op("scalar",
                      lambda E: E.activation(rstd_row[:], rstd_row[:], AF.Exp),
                      reads=[rrstd], writes=[rrstd])
                em.op("vector",
                      lambda E: E.tensor_mul(negm_row[:], negm_row[:],
                                             rstd_row[:]),
                      reads=[rrstd, rnegm], writes=[rnegm])
                for tc in range(NTC):
                    sl = slice(tc * 512, (tc + 1) * 512)
                    em.op("tensor",
                          lambda E, s=sl: E.matmul(
                              bcA_ps[:], ones128[:], rstd_row[:, s],
                              start=True, stop=True, skip_group_check=True),
                          reads=[rrstd, rconst], writes=[rbA])
                    em.op("tensor",
                          lambda E, s=sl: E.matmul(
                              bcB_ps[:], nones128[:], negm_row[:, s],
                              start=True, stop=True, skip_group_check=True),
                          reads=[rnegm, rconst], writes=[rbB])
                    for dc in range(DC):
                        em.op("vector",
                              lambda E, d=dc, s=sl: E.tensor_mul(
                                  app_t[:], Xf[d][:, s], bcA_ps[:]),
                              reads=[rXf[dc], rbA], writes=[rapp])
                        em.op("vector",
                              lambda E: E.tensor_add(app_t[:], app_t[:],
                                                     bcB_ps[:]),
                              reads=[rbB, rapp], writes=[rapp])
                        em.op("scalar",
                              lambda E, d=dc, s=sl, cg=cG, sg=sG:
                              E.activation(
                                  Hb[d][:, s], app_t[:], AF.Identity,
                                  bias=mod_sb[:, sg + d:sg + d + 1],
                                  scale=mod_sb[:, cg + d:cg + d + 1]),
                              reads=[rapp, rmod], writes=[rHb[dc]])
                    if post_tc is not None:
                        post_tc(tc)
                if do_flush:
                    em.flush()

        def _dump_and_exit(aps_res):
            rout = Res()
            evs = []
            for (dst, src, rr, eng) in aps_res:
                evs.append(dma(dst, src, writes=[rout], reads=[rr],
                               eng=eng, sem="so"))
            for ev in evs:
                em.wait_event("sync", ev)
            em.flush()
            em.finalize()

        if debug_stop == "mod":
            _dump_and_exit([(y_h.ap()[0:128, 0:6 * DC], mod_sb[:], rmod,
                             "sync")])
            return nc

        ph2 = ExitStack()
        if True:
            wq2b = [sbuf(f"wq2b{i}", [128, DC * 128], BF16, ph2)
                    for i in range(2)]
            wk2b = [sbuf(f"wk2b{i}", [128, DC * 128], BF16, ph2)
                    for i in range(2)]
            wv2b = [sbuf(f"wv2b{i}", [128, DC * 128], BF16, ph2)
                    for i in range(2)]
            rwq = [Res(), Res()]
            rwk = [Res(), Res()]
            rwv = [Res(), Res()]
            z_flat = [sbuf(f"zfl{i}", [128, S], BF16, ph2) for i in range(FC)]
            rz = [Res() for _ in range(FC)]
            Qall = [sbuf(f"Qall{i}", [128, S], BF16, ph2) for i in range(2)]
            Kall = [sbuf(f"Kall{i}", [128, S], BF16, ph2) for i in range(2)]
            Vb = [sbuf(f"Vb{i}", [128, NB * 130], BF16, ph2)
                  for i in range(2)]
            rQ = [Res(), Res()]
            rK = [Res(), Res()]
            rV = [Res(), Res()]
            PT = [sbuf(f"PT{i}", [128, 1024], BF16, ph2) for i in range(3)]
            rPT = [Res(), Res(), Res()]
            rcp_sb = sbuf("rcp_sb", [65, 4096], BF16, ph2)
            rrs = [[Res(), Res()] for _ in range(NTC)]
            nrm_bf = sbuf("nrm_bf", [128, 512], BF16, ph2); rnb = Res()

            ut_ps = [psum(f"ut_ps{i}", [128, 512], ph2) for i in range(2)]
            rut = [Res(), Res()]

            def load_attn_w(fc):
                b = fc % 2
                for (nm, dst, src, rr) in (("q", wq2b[b], wq2_h, rwq[b]),
                                           ("k", wk2b[b], wk2_h, rwk[b]),
                                           ("v", wv2b[b], wv2_h, rwv[b])):
                    dma(dst[:].rearrange("p (c n) -> p c n", c=DC),
                        src.ap()[fc].rearrange("c p n -> p c n"),
                        writes=[rr], eng="gpsimd", sem=f"dw{nm}{b}")

            load_attn_w(0)
            if FC > 1:
                load_attn_w(1)

            uctr = [0]

            def proj_units(fcn):
                par = fcn % 2
                units = []
                for (dstT, rdst, wsrc, rw, scl) in (
                        (Qall[par], rQ[par], wq2b[par], rwq[par], 0.125),
                        (Kall[par], rK[par], wk2b[par], rwk[par], 1.0)):
                    for tc in range(NTC):
                        def u(dstT=dstT, rdst=rdst, wsrc=wsrc, rw=rw,
                              scl=scl, tc=tc):
                            sl = slice(tc * 512, (tc + 1) * 512)
                            ub = uctr[0] % 2
                            uctr[0] += 1
                            mmg([(ut_ps[ub][:],
                                  wsrc[:, dc * 128:(dc + 1) * 128],
                                  Hb[dc][:, sl]) for dc in range(DC)],
                                writes=[rut[ub]], reads=[rw] + rHb)
                            em.op("vector",
                                  lambda E, d=dstT, s=sl, sc=scl, ub=ub:
                                  E.tensor_scalar_mul(d[:, s], ut_ps[ub][:],
                                                      sc),
                                  reads=[rut[ub]], writes=[rdst])
                        units.append(u)
                for tb in range(NB):
                    def u(tb=tb, par=par):
                        ub = uctr[0] % 2
                        uctr[0] += 1
                        mmg([(ut_ps[ub][:, 0:128],
                              Hb[dc][:, tb * 128:(tb + 1) * 128],
                              wv2b[par][:, dc * 128:(dc + 1) * 128])
                             for dc in range(DC)],
                            writes=[rut[ub]], reads=[rwv[par]] + rHb)
                        em.op("vector",
                              lambda E, t=tb, ub=ub, par=par: E.tensor_copy(
                                  Vb[par][:, t * 130:t * 130 + 64],
                                  ut_ps[ub][:, 0:64]),
                              reads=[rut[ub]], writes=[rV[par]])
                        em.op("vector",
                              lambda E, t=tb, ub=ub, par=par: E.tensor_copy(
                                  Vb[par][:, t * 130 + 65:t * 130 + 129],
                                  ut_ps[ub][:, 64:128]),
                              reads=[rut[ub]], writes=[rV[par]])
                    units.append(u)
                return units

            for par in range(2):
                em.op("vector",
                      lambda E, par=par: E.memset(Vb[par][:], 1.0),
                      writes=[rV[par]])

            p0units = proj_units(0)
            p0i = [0]

            def ln1_post(tc):
                n = len(p0units) * (tc + 1) // NTC
                while p0i[0] < n:
                    p0units[p0i[0]]()
                    p0i[0] += 1

            emit_ln(CA, 0, "a", post_tc=ln1_post)

            if debug_stop == "ln1":
                _dump_and_exit([(y_h.ap()[dc * 128:(dc + 1) * 128, :],
                                 Hb[dc][:], rHb[dc], "gpsimd")
                                for dc in range(DC)])
                return nc

            s2_ps = [psum(f"s2_ps{i}", [128, 1024], ph2)
                     for i in range(2)]
            rs2 = [Res(), Res()]
            z2_ps = [psum(f"z2_ps{i}", [128, 512], ph2) for i in range(2)]
            rz2 = [Res(), Res()]
            zstage = sbuf("zstage", [64, 512], BF16, ph2); rzs = Res()
            lsb = sbuf("lsb", [68, 512], F32, ph2); rlw = Res()
            HT = FC * DC // 2
            wob = [sbuf(f"wob{i}", [128, HT * 128], BF16, ph2)
                   for i in range(2)]
            rwo = [Res(), Res()]
            for i in range(2):
                dma(wob[i][:].rearrange("p (c n) -> p c n", c=HT),
                    wo_h.ap()[i * HT:(i + 1) * HT]
                    .rearrange("c p n -> p c n"),
                    writes=[rwo[i]], eng="gpsimd", sem="dwo")
            while p0i[0] < len(p0units):
                p0units[p0i[0]]()
                p0i[0] += 1

            for fc in range(FC):
                b = fc % 2
                if fc + 2 < FC:
                    load_attn_w(fc + 2)
                units = proj_units(fc + 1) if fc + 1 < FC else []
                ui = 0
                ngr_total = 2 * sum(2 * qc + 2 for qc in range(NTC))
                gctr = 0
                for hh in range(2):
                    pr = slice(hh * 64, (hh + 1) * 64)
                    zr = slice(hh * 64, hh * 64 + 64)
                    for qc in range(NTC):
                        qsl = slice(qc * 512, (qc + 1) * 512)
                        npc = 4 * qc + 4
                        groups = [(g, min(g + 2, npc))
                                  for g in range(0, npc, 2)]

                        def emit_S(gi):
                            p0, p1 = groups[gi]
                            sb_, rsb = s2_ps[gi % 2], rs2[gi % 2]
                            pt, rpt = PT[gi % 3], rPT[gi % 3]
                            lo = {}
                            for pc in range(p0, p1):
                                h0 = (pc - p0) * 512
                                col_lo = max(0, pc * 128 - qc * 512)
                                lo[pc] = col_lo
                                cs = slice(qc * 512 + col_lo, (qc + 1) * 512)
                                em.op("tensor",
                                      lambda E, p=sb_, c=col_lo, k=pc, q=cs,
                                      h=pr, o=h0, bb=b:
                                      E.matmul(
                                          p[:, o + c:o + 512],
                                          Kall[bb][h, k * 128:(k + 1) * 128],
                                          Qall[bb][h, q], start=True,
                                          stop=True, skip_group_check=True),
                                      reads=[rK[b], rQ[b]], writes=[rsb])
                                if pc * 128 >= qc * 512:
                                    em.op("vector",
                                          lambda E, p=sb_, c=col_lo, o=h0:
                                          E.tensor_add(
                                              p[:, o + c:o + c + 128],
                                              p[:, o + c:o + c + 128],
                                              diagm[:]),
                                          reads=[rconst], writes=[rsb])
                            if (p1 - p0 == 2 and lo[p0] == 0
                                    and lo[p0 + 1] == 0):
                                em.op("scalar",
                                      lambda E, p=sb_, t=pt: E.activation(
                                          t[:, 0:1024], p[:, 0:1024],
                                          AF.Exp),
                                      reads=[rsb], writes=[rpt])
                            else:
                                for pc in range(p0, p1):
                                    h0 = (pc - p0) * 512
                                    c = lo[pc]
                                    em.op("scalar",
                                          lambda E, p=sb_, t=pt, c=c, o=h0:
                                          E.activation(
                                              t[:, o + c:o + 512],
                                              p[:, o + c:o + 512], AF.Exp),
                                          reads=[rsb], writes=[rpt])
                            return lo

                        def emit_PV(gi, lo):
                            p0, p1 = groups[gi]
                            pt, rpt = PT[gi % 3], rPT[gi % 3]
                            for pc in range(p0, p1):
                                h0 = (pc - p0) * 512
                                c = lo[pc]
                                em.op("tensor",
                                      lambda E, c=c, k=pc, t=pt, o=h0,
                                      a=(pc == 0), z=(pc == npc - 1),
                                      h=hh, bb=b:
                                      E.matmul(
                                          z2_ps[h][0:65, c:512],
                                          Vb[bb][:, k * 130 + h * 65:
                                                 k * 130 + h * 65 + 65],
                                          t[:, o + c:o + 512], start=a,
                                          stop=z, skip_group_check=True),
                                      reads=[rpt, rV[b]],
                                      writes=([rz2[hh]]
                                              if pc in (0, npc - 1)
                                              else ()))

                        prev = None
                        for gi in range(len(groups)):
                            lo = emit_S(gi)
                            if prev is not None:
                                emit_PV(prev[0], prev[1])
                            prev = (gi, lo)
                            gctr += 1
                            while (units and
                                   ui < len(units) * gctr // ngr_total):
                                units[ui]()
                                ui += 1
                        emit_PV(prev[0], prev[1])
                        if hh == 0:
                            em.op("vector",
                                  lambda E, f=fc, q=qsl: E.tensor_copy(
                                      z_flat[f][0:64, q],
                                      z2_ps[0][0:64, :]),
                                  reads=[rz2[0]], writes=[rz[fc]])
                        else:
                            em.op("vector",
                                  lambda E: E.tensor_copy(
                                      zstage[:], z2_ps[1][0:64, :]),
                                  reads=[rz2[1]], writes=[rzs])
                            dma(z_flat[fc][64:128, qsl], zstage[:],
                                writes=[rz[fc]], reads=[rzs], sem="dzs")
                        em.op("vector",
                              lambda E, h=hh: E.tensor_copy(
                                  lsb[64:65, :], z2_ps[h][64:65, :]),
                              reads=[rz2[hh]], writes=[rlw])
                        if qc != NTC - 1:
                            dma(lsb[65 + qc:66 + qc, :], lsb[64:65, :],
                                writes=[rlw], reads=[rlw], sem="dls")
                    em.op("scalar",
                          lambda E: E.activation(
                              lsb[64:64 + NTC, :], lsb[64:64 + NTC, :],
                              AF.Ln),
                          reads=[rlw], writes=[rlw])
                    em.op("vector",
                          lambda E: E.tensor_scalar_mul(
                              lsb[64:64 + NTC, :], lsb[64:64 + NTC, :],
                              -1.0),
                          reads=[rlw], writes=[rlw])
                    em.op("scalar",
                          lambda E: E.activation(
                              nrm_bf[64:64 + NTC, :],
                              lsb[64:64 + NTC, :], AF.Exp),
                          reads=[rlw], writes=[rnb])
                    for t2 in range(NTC):
                        sr = 64 if t2 == NTC - 1 else 65 + t2
                        dma(rcp_sb[64:65, t2 * 1024 + hh * 512:
                                   t2 * 1024 + hh * 512 + 512],
                            nrm_bf[sr:sr + 1, :],
                            writes=[rrs[t2][hh]], reads=[rnb],
                            sem=f"dlo{t2}{hh}")
                while ui < len(units):
                    units[ui]()
                    ui += 1
                for tc in range(NTC):
                    sl = slice(tc * 512, (tc + 1) * 512)
                    ub = uctr[0] % 2
                    uctr[0] += 1
                    em.op("tensor",
                          lambda E, t=tc, ub=ub: E.matmul(
                              ut_ps[ub][:], e65[64:65, 0:128],
                              rcp_sb[64:65, t * 1024:t * 1024 + 512],
                              start=True, stop=False, skip_group_check=True),
                          reads=[rrs[tc][0], rconst], writes=[rut[ub]])
                    em.op("tensor",
                          lambda E, t=tc, ub=ub: E.matmul(
                              ut_ps[ub][:], e65[64:65, 128:256],
                              rcp_sb[64:65, t * 1024 + 512:
                                     (t + 1) * 1024],
                              start=False, stop=True, skip_group_check=True),
                          reads=[rrs[tc][1], rconst], writes=[rut[ub]])
                    em.op("scalar",
                          lambda E, ub=ub: E.copy(nrm_bf[:], ut_ps[ub][:]),
                          reads=[rut[ub]], writes=[rnb])
                    em.op("vector",
                          lambda E, f=fc, s=sl: E.tensor_mul(
                              z_flat[f][:, s], z_flat[f][:, s], nrm_bf[:]),
                          reads=[rnb, rz[fc]], writes=[rz[fc]])

            for tc in range(NTC):
                sl = slice(tc * 512, (tc + 1) * 512)
                for dc in range(DC):
                    ub = uctr[0] % 2
                    uctr[0] += 1
                    mmg([(ut_ps[ub][:],
                          wob[(fc * DC + dc) // HT]
                          [:, ((fc * DC + dc) % HT) * 128:
                           ((fc * DC + dc) % HT + 1) * 128],
                          z_flat[fc][:, sl]) for fc in range(FC)],
                        writes=[rut[ub]], reads=rwo + rz)
                    em.op("vector",
                          lambda E, d=dc, s=sl, ub=ub:
                          E.scalar_tensor_tensor(
                              Xf[d][:, s], ut_ps[ub][:],
                              mod_sb[:, GA + d:GA + d + 1], Xf[d][:, s],
                              ALU.mult, ALU.add),
                          reads=[rut[ub], rmod, rXf[dc]], writes=[rXf[dc]])
            em.flush()
            ph2.close()

        if debug_stop == "attn":
            _dump_and_exit([(y_h.ap()[dc * 128:(dc + 1) * 128, :], Xf[dc][:],
                             rXf[dc], "sync") for dc in range(DC)])
            return nc

        with ExitStack() as ph4:
            NSC = CAP // 128
            gw_bf = sbuf("gw_bf", [128, DC * E], BF16, ph4); rgw = Res()
            mx_sb = sbuf("mx_sb", [128, 1], F32, ph4); rmx = Res()
            sc_sb = sbuf("sc_sb", [128, 1], F32, ph4); rsc = Res()
            eq_sb = sbuf("eq_sb", [128, E], F32, ph4); req = Res()
            m2_sb = sbuf("m2_sb", [128, 1], F32, ph4); rm2 = Res()
            rank16 = sbuf("rank16", [128, NB * E], mybir.dt.float16, ph4)
            rrk16 = Res()
            rank_row = sbuf("rank_row", [1, 512], mybir.dt.float16, ph4)
            rrow = Res()

            dma(gw_bf[:].rearrange("p (c n) -> p c n", c=DC),
                gw_h.ap().rearrange("c p n -> p c n"),
                writes=[rgw], eng="gpsimd", sem="dgw")

            lnps = ExitStack()
            lg_ps = lnps.enter_context(nc.psum_tensor("lg_ps", [128, E], F32))
            rlg = Res()

            def router_tc(tc):
                for t4 in range(4):
                    tb = tc * 4 + t4
                    if tb >= NB:
                        break
                    mmg([(lg_ps[:, 0:E], Hb[dc][:, tb * 128:(tb + 1) * 128],
                          gw_bf[:, dc * E:(dc + 1) * E])
                         for dc in range(DC)],
                        writes=[rlg], reads=[rgw] + rHb)
                    em.op("vector",
                          lambda E_, NE=E: E_.reduce_max(
                              mx_sb[:], lg_ps[:, 0:NE], AX.X),
                          reads=[rlg], writes=[rmx])
                    em.op("vector",
                          lambda E_, NE=E: E_.tensor_scalar(
                              eq_sb[:], lg_ps[:, 0:NE], mx_sb[:], None,
                              ALU.is_equal),
                          reads=[rlg, rmx], writes=[req])
                    em.op("vector",
                          lambda E: E.tensor_mul(eq_sb[:], eq_sb[:], w8t[:]),
                          reads=[req, rconst], writes=[req])
                    em.op("vector",
                          lambda E: E.reduce_max(m2_sb[:], eq_sb[:], AX.X),
                          reads=[req], writes=[rm2])
                    em.op("vector",
                          lambda E_, t=tb, NE=E: E_.tensor_scalar(
                              oh_bf[:, t * NE:(t + 1) * NE], eq_sb[:],
                              m2_sb[:], None, ALU.is_equal),
                          reads=[req, rm2], writes=[rohb])
                    em.op("vector",
                          lambda E_, t=tb, NE=E: E_.tensor_scalar(
                              oh_f[:, t * NE:(t + 1) * NE], eq_sb[:],
                              m2_sb[:], None, ALU.is_equal),
                          reads=[req, rm2], writes=[rohf])
                    em.op("scalar",
                          lambda E: E.activation(sc_sb[:], mx_sb[:],
                                                 AF.Sigmoid),
                          reads=[rmx], writes=[rsc])
                    em.op("vector",
                          lambda E_, t=tb: E_.tensor_copy(
                              score_f[:, t:t + 1], sc_sb[:]),
                          reads=[rsc], writes=[rscf])

            emit_ln(CB, SB_, "b", post_tc=router_tc, do_flush=False)
            em.flush()
            lnps.close()
            NGB = 3
            wg_ch = [sbuf(f"wg_ch{i}", [128, DC * 128], BF16, ph4)
                     for i in range(NGB)]
            wu_ch = [sbuf(f"wu_ch{i}", [128, DC * 128], BF16, ph4)
                     for i in range(NGB)]
            rwg = [Res() for _ in range(NGB)]
            rwu = [Res() for _ in range(NGB)]
            wd_bf = [sbuf(f"wd_bf{i}", [128, D], BF16, ph4)
                     for i in range(2)]
            rwd = [Res() for _ in range(2)]
            act_sl = sbuf("act_sl", [128, IC * 512], BF16, ph4); ract = Res()
            silu_sb = [sbuf("silu0", [128, 512], F32, ph4)] * 2
            rsilu0 = Res()
            rsilu = [rsilu0, rsilu0]
            pp = [psum(f"pp{i}", [128, 512], ph4) for i in range(8)]
            rpp = [Res() for _ in range(8)]

            if debug_stop == "ln2":
                em.flush()
                _dump_and_exit([(y_h.ap()[dc * 128:(dc + 1) * 128, :],
                                 Hb[dc][:], rHb[dc], "gpsimd")
                                for dc in range(DC)])
                return nc

            rank_ps = pp[6]
            rrkp = rpp[6]
            for tb in range(NB):
                for tb2 in range(tb + 1):
                    em.op("tensor",
                          lambda E_, t=tb, t2=tb2, NE=E: E_.matmul(
                              rank_ps[:, t * NE:(t + 1) * NE],
                              (ustrict if t2 == t else ufull)[:],
                              oh_bf[:, t2 * NE:(t2 + 1) * NE],
                              start=(t2 == 0 and t == 0),
                              stop=(t2 == t and t == NB - 1),
                              skip_group_check=True),
                          reads=([rohb, rconst] if tb2 in (0, tb) else ()),
                          writes=([rrkp] if tb2 in (0, tb) else ()))
            em.op("scalar",
                  lambda E_, NE=E: E_.copy(rank_f[:],
                                           rank_ps[:, 0:NB * NE]),
                  reads=[rrkp], writes=[rrk])
            em.op("vector",
                  lambda E: E.tensor_scalar_add(rank_f[:], rank_f[:], 1.0),
                  reads=[rrk], writes=[rrk])
            em.op("vector",
                  lambda E: E.tensor_mul(rank_f[:], rank_f[:], oh_f[:]),
                  reads=[rrk, rohf], writes=[rrk])
            em.op("vector",
                  lambda E: E.tensor_scalar_add(rank_f[:], rank_f[:], -1.0),
                  reads=[rrk], writes=[rrk])
            em.op("vector",
                  lambda E: E.tensor_copy(rank16[:], rank_f[:]),
                  reads=[rrk], writes=[rrk16])

            if debug_stop == "router":
                em.flush()
                _dump_and_exit([(y_h.ap()[0:128, 0:NB * E], rank_f[:], rrk,
                                 "sync")])
                return nc

            def load_guchunk(ppe, prouted, src, dst, rr, ic, csem):
                hf = ic // (IC // 2)
                cc = (ic % (IC // 2)) * 128
                ap = (src.ap()[ppe, hf] if prouted else src.ap()[hf])
                dma(dst[:].rearrange("p (c n) -> p c n", c=DC),
                    ap.rearrange("c p n -> p c n")[:, :, cc:cc + 128],
                    writes=[rr], eng="gpsimd", sem=csem)

            def load_wd(ppe, prouted, d_h, ic):
                src = d_h.ap()[ppe] if prouted else d_h.ap()
                dma(wd_bf[ic % 2][:], src[ic * 128:(ic + 1) * 128, :],
                    writes=[rwd[ic % 2]], eng="gpsimd", sem=f"dwd{ic % 2}")

            shsc = ExitStack()
            gres = sbuf("gres", [128, 2 * DC * IDIM], BF16, shsc)
            rgres = Res()
            for mi, src in ((0, shg_h), (1, shu_h)):
                for hf in range(2):
                    dma(gres[:].rearrange("p (m c n) -> p m c n",
                                          m=2, c=DC)[:, mi]
                        [:, :, hf * IH:(hf + 1) * IH],
                        src.ap()[hf].rearrange("c p n -> p c n"),
                        writes=[rgres], eng="gpsimd", sem=f"dgrs{mi}")
            for slab in range(NTC):
                sl = slice(slab * 512, (slab + 1) * 512)
                for ic in range(IC):
                    pg, pu = pp[(ic % 2) * 2], pp[(ic % 2) * 2 + 1]
                    rpg, rpu = rpp[(ic % 2) * 2], rpp[(ic % 2) * 2 + 1]
                    mmg([(pg[:],
                          gres[:, dc * IDIM + ic * 128:
                               dc * IDIM + (ic + 1) * 128],
                          Hb[dc][:, sl]) for dc in range(DC)],
                        writes=[rpg], reads=[rgres] + rHb)
                    mmg([(pu[:],
                          gres[:, DC * IDIM + dc * IDIM + ic * 128:
                               DC * IDIM + dc * IDIM + (ic + 1) * 128],
                          Hb[dc][:, sl]) for dc in range(DC)],
                        writes=[rpu], reads=[rgres] + rHb)
                    em.op("scalar",
                          lambda E, p=pg, i=ic: E.activation(
                              silu_sb[i % 2][:], p[:], AF.Silu),
                          reads=[rpg], writes=[rsilu[ic % 2]])
                    em.op("vector",
                          lambda E, i=ic, p=pu: E.tensor_mul(
                              act_sl[:, i * 512:(i + 1) * 512],
                              silu_sb[i % 2][:], p[:]),
                          reads=[rsilu[ic % 2], rpu], writes=[ract])
                for ic in range(IC):
                    load_wd("sh", False, shd_h, ic)
                    for dc in range(DC):
                        em.op("tensor",
                              lambda E, i=ic, d=dc, b2=ic % 2,
                              a=(ic == 0), z=(ic == IC - 1):
                              E.matmul(pp[d][:],
                                       wd_bf[b2][:, d * 128:(d + 1) * 128],
                                       act_sl[:, i * 512:(i + 1) * 512],
                                       start=a, stop=z,
                                       skip_group_check=True),
                              reads=[rwd[ic % 2], ract],
                              writes=([rpp[dc]] if ic in (0, IC - 1)
                                      else ()))
                for dc in range(DC):
                    em.op("vector",
                          lambda E, d=dc, s=sl: E.scalar_tensor_tensor(
                              Xf[d][:, s], pp[d][:],
                              mod_sb[:, GB + d:GB + d + 1], Xf[d][:, s],
                              ALU.mult, ALU.add),
                          reads=[rpp[dc], rmod, rXf[dc]], writes=[rXf[dc]])
            em.flush()
            shsc.close()
            h2tm = sbuf("h2tm", [128, NB * D], BF16, ph4); rh2t = Res()
            Pg_buf = sbuf("Pg_buf", [128, NB * CAP], BF16, ph4); rPg = Res()
            Ps_buf = sbuf("Ps_buf", [128, NSC * S], BF16, ph4); rPs = Res()
            Xg = sbuf("Xg", [128, DC * CAP], BF16, ph4); rXg = Res()
            Yg = sbuf("Yg", [128, NSC * D], BF16, ph4); rYg = Res()

            for tb in range(NB):
                for dc in range(DC):
                    pb = pp[(tb * DC + dc) % 2]
                    rpb = rpp[(tb * DC + dc) % 2]
                    em.op("tensor",
                          lambda E, t=tb, d=dc, p=pb: E.transpose(
                              p[:, 0:64].bitcast(BF16),
                              Hb[d][:, t * 128:(t + 1) * 128],
                              identb[:]),
                          reads=[rHb[dc], rconst], writes=[rpb])
                    em.op("vector",
                          lambda E, t=tb, d=dc, p=pb: E.tensor_copy(
                              h2tm[:, t * D + d * 128:t * D + (d + 1) * 128],
                              p[:, 0:64].bitcast(BF16)),
                          reads=[rpb], writes=[rh2t])

            def build_Pg(e):
                for tb in range(NB):
                    em.op("vector",
                          lambda E_, t=tb, e=e, NE=E: E_.tensor_scalar(
                              Pg_buf[:, t * CAP:t * CAP + CAP],
                              iotaf[:, 0:CAP],
                              rank_f[:, t * NE + e:t * NE + e + 1],
                              score_f[:, t:t + 1],
                              ALU.is_equal, ALU.mult),
                          reads=[rrk, rscf, rconst], writes=[rPg])

            def build_Ps_tc(e, tc):
                if True:
                    for t4 in range(4):
                        tb = tc * 4 + t4
                        em.op("tensor",
                              lambda E_, t=tb, t4=t4, e=e, NE=E: E_.matmul(
                                  pp[0][0:1, t4 * 128:(t4 + 1) * 128],
                                  rank16[:, t * NE + e:t * NE + e + 1],
                                  identh[:], start=True, stop=True,
                                  skip_group_check=True),
                              reads=[rrk16, rconst], writes=[rpp[0]])
                    em.op("scalar",
                          lambda E_: E_.copy(rank_row[:], pp[0][0:1, :]),
                          reads=[rpp[0]], writes=[rrow])
                if True:
                    em.op("tensor",
                          lambda E_: E_.matmul(
                              pp[1][:], onesh[:], rank_row[:],
                              start=True, stop=True, skip_group_check=True),
                          reads=[rrow, rconst], writes=[rpp[1]])
                    for sc in range(NSC):
                        em.op("vector",
                              lambda E_, t=tc, s2=sc: E_.tensor_scalar(
                                  Ps_buf[:, (s2 * NTC + t) * 512:
                                         (s2 * NTC + t + 1) * 512],
                                  pp[1][:], iotasc[:, s2:s2 + 1], None,
                                  ALU.is_equal),
                              reads=[rpp[1], rconst], writes=[rPs])

            def gather_unit(e_, dc):
                mmg([(pp[dc % 2][:, 0:CAP],
                      h2tm[:, t * D + dc * 128:t * D + (dc + 1) * 128],
                      Pg_buf[:, t * CAP:t * CAP + CAP])
                     for t in range(NB)],
                    writes=[rpp[dc % 2]], reads=[rh2t, rPg])
                em.op("vector",
                      lambda E_, d=dc: E_.tensor_copy(
                          Xg[:, d * CAP:(d + 1) * CAP],
                          pp[d % 2][:, 0:CAP]),
                      reads=[rpp[dc % 2]], writes=[rXg])

            build_Pg(0)
            for dc in range(DC):
                gather_unit(0, dc)
            for e in range(E):
                if e == 0:
                    load_guchunk(e, True, rtg_h, wg_ch[0], rwg[0], 0, "dwg0")
                    load_guchunk(e, True, rtu_h, wu_ch[0], rwu[0], 0, "dwu0")
                for ic in range(IC):
                    cb_ = ic % NGB
                    for pf in ((1, 2) if ic == 0 else (2,)):
                        if ic + pf < IC:
                            load_guchunk(e, True, rtg_h,
                                         wg_ch[(ic + pf) % NGB],
                                         rwg[(ic + pf) % NGB], ic + pf,
                                         f"dwg{(ic + pf) % NGB}")
                            load_guchunk(e, True, rtu_h,
                                         wu_ch[(ic + pf) % NGB],
                                         rwu[(ic + pf) % NGB], ic + pf,
                                         f"dwu{(ic + pf) % NGB}")
                    pg, pu = pp[2 + (ic % 2)], pp[4 + (ic % 2)]
                    rpg, rpu = rpp[2 + (ic % 2)], rpp[4 + (ic % 2)]
                    if ic % 4 == 3:
                        build_Ps_tc(e, ic // 4)
                    mmg([(pg[:, 0:CAP],
                          wg_ch[cb_][:, dc * 128:(dc + 1) * 128],
                          Xg[:, dc * CAP:(dc + 1) * CAP])
                         for dc in range(DC)],
                        writes=[rpg], reads=[rwg[cb_], rXg])
                    mmg([(pu[:, 0:CAP],
                          wu_ch[cb_][:, dc * 128:(dc + 1) * 128],
                          Xg[:, dc * CAP:(dc + 1) * CAP])
                         for dc in range(DC)],
                        writes=[rpu], reads=[rwu[cb_], rXg])
                    em.op("scalar",
                          lambda E, p=pg, i=ic: E.activation(
                              silu_sb[i % 2][:, 0:CAP], p[:, 0:CAP],
                              AF.Silu),
                          reads=[rpg], writes=[rsilu[ic % 2]])
                    em.op("vector",
                          lambda E, i=ic, p=pu: E.tensor_mul(
                              act_sl[:, i * 512:i * 512 + CAP],
                              silu_sb[i % 2][:, 0:CAP], p[:, 0:CAP]),
                          reads=[rsilu[ic % 2], rpu], writes=[ract])
                if e + 1 < E:
                    build_Pg(e + 1)
                gu = list(range(DC)) if e + 1 < E else []
                gui = 0
                for ic in range(IC):
                    load_wd(e, True, rtd_h, ic)
                    for sc in range(NSC):
                        for dh in range(2):
                            k = sc * 2 + dh
                            em.op("tensor",
                                  lambda E, i=ic, s2=sc, d2=dh, b2=ic % 2,
                                  k=k, a=(ic == 0), z=(ic == IC - 1):
                                  E.matmul(
                                      pp[2 + k][:],
                                      act_sl[:, i * 512 + s2 * 128:
                                             i * 512 + (s2 + 1) * 128],
                                      wd_bf[b2][:, d2 * 512:(d2 + 1) * 512],
                                      start=a, stop=z,
                                      skip_group_check=True),
                                  reads=[rwd[ic % 2], ract],
                                  writes=([rpp[2 + k]] if ic in (0, IC - 1)
                                          else ()))
                    while gui < len(gu) * (ic + 1) // IC:
                        gather_unit(e + 1, gu[gui])
                        gui += 1
                    if ic == IC - 4 and e + 1 < E:
                        load_guchunk(e + 1, True, rtg_h, wg_ch[0], rwg[0],
                                     0, "dwg0")
                        load_guchunk(e + 1, True, rtu_h, wu_ch[0], rwu[0],
                                     0, "dwu0")
                for k in range(NSC * 2):
                    em.op("vector",
                          lambda E_, k=k: E_.tensor_copy(
                              Yg[:, k * 512:(k + 1) * 512], pp[2 + k][:]),
                          reads=[rpp[2 + k]], writes=[rYg])
                for dc in range(DC):
                    for tc in range(NTC):
                        k = 6 + (dc * NTC + tc) % 2
                        chain = [(pp[k][:],
                                  Yg[:, sc * D + dc * 128:
                                     sc * D + (dc + 1) * 128],
                                  Ps_buf[:, (sc * NTC + tc) * 512:
                                         (sc * NTC + tc + 1) * 512])
                                 for sc in range(NSC)]
                        mmg(chain, writes=[rpp[k]], reads=[rYg, rPs])
                        em.op("vector",
                              lambda E_, d=dc, t=tc, k=k:
                              E_.scalar_tensor_tensor(
                                  Xf[d][:, t * 512:(t + 1) * 512],
                                  pp[k][:],
                                  mod_sb[:, GB + d:GB + d + 1],
                                  Xf[d][:, t * 512:(t + 1) * 512],
                                  ALU.mult, ALU.add),
                              reads=[rpp[k], rmod, rXf[dc]],
                              writes=[rXf[dc]])
            em.flush()

        rout = Res()
        evs = []
        for dc in range(DC):
            evs.append(dma(y_h.ap()[dc * 128:(dc + 1) * 128, :], Xf[dc][:],
                           writes=[rout], reads=[rXf[dc]], sem="so"))
        for ev in evs:
            em.wait_event("sync", ev)
        em.flush()
        em.finalize()

    return nc


def _pack(inputs, S, NH, IDIM, E):
    f = np.ascontiguousarray
    FC = NH // 2
    IH = IDIM // 2
    wq = inputs["L0_attn_W_Q"].transpose(0, 2, 1).reshape(NH, DC, 128, 64)
    wk = inputs["L0_attn_W_K"].transpose(0, 2, 1).reshape(NH, DC, 128, 64)
    wv = inputs["L0_attn_W_V"].transpose(0, 2, 1).reshape(NH, DC, 128, 64)

    def pair(w):
        return f(w.reshape(FC, 2, DC, 128, 64).transpose(0, 2, 3, 1, 4)
                 .reshape(FC, DC, 128, 128))

    wo = f(np.asarray(inputs["L0_attn_W_O"]).T.reshape(FC, 128, DC, 128)
           .transpose(0, 2, 1, 3).reshape(FC * DC, 128, 128))
    gw = f(np.asarray(inputs["L0_ffn_gate_w"]).reshape(DC, 128, E))

    def halves(w):
        return f(np.asarray(w).reshape(DC, 128, 2, IH).transpose(2, 0, 1, 3))

    out = {
        "wmod": f(np.asarray(inputs["L0_wmod"])),
        "wq2": pair(wq), "wk2": pair(wk), "wv2": pair(wv), "wo": wo,
        "gw": gw,
        "shg": halves(inputs["L0_ffn_sh_gate"]),
        "shu": halves(inputs["L0_ffn_sh_up"]),
        "shd": f(np.asarray(inputs["L0_ffn_sh_down"])),
        "rtg": f(np.stack([halves(inputs["L0_ffn_rt_gate"][e])
                           for e in range(E)])),
        "rtu": f(np.stack([halves(inputs["L0_ffn_rt_up"][e])
                           for e in range(E)])),
        "rtd": f(np.asarray(inputs["L0_ffn_rt_down"])),
    }
    i = np.arange(128)
    out["diagm"] = np.where(i[:, None] > i[None, :], NEG, 0.0).astype(
        np.float32)
    out["ones128"] = np.ones((1, 128), np.float32)
    out["nones128"] = -np.ones((1, 128), np.float32)
    out["onescol"] = np.ones((128, 1), ml_dtypes.bfloat16)
    eab = np.zeros((1, 256), ml_dtypes.bfloat16)
    eab[0, 0:64] = 1
    eab[0, 192:256] = 1
    out["eab"] = eab
    out["ones128b"] = np.ones((1, 128), ml_dtypes.bfloat16)
    out["identb"] = np.eye(128, dtype=ml_dtypes.bfloat16)
    out["identf"] = np.eye(128, dtype=np.float32)
    out["w8"] = np.broadcast_to(
        (2.0 ** np.arange(E + 1, 1, -1)).astype(np.float32), (128, E)).copy()
    out["ufull"] = np.ones((128, 128), ml_dtypes.bfloat16)
    out["ustrict"] = np.triu(np.ones((128, 128)), k=1).astype(
        ml_dtypes.bfloat16)
    out["iotaf"] = np.broadcast_to(np.arange(512, dtype=np.float32),
                                   (128, 512)).copy()
    CAP = 384
    out["iotasc"] = (np.arange(128, dtype=np.float32)[:, None] +
                     128.0 * np.arange(CAP // 128, dtype=np.float32)[None, :]
                     ).copy()
    out["identh"] = np.eye(128, dtype=np.float16)
    out["onesh"] = np.ones((1, 128), np.float16)
    e65 = np.zeros((68, 256), ml_dtypes.bfloat16)
    e65[64:68, 0:64] = 1
    e65[64:68, 192:256] = 1
    out["e65"] = e65
    return out


def kernel(**inputs):
    S, NH, IDIM, E = 2048, 16, 2048, 8
    B = np.asarray(inputs["L0_attn_x"]).shape[0]
    shared = _pack(inputs, S, NH, IDIM, E)
    x = np.asarray(inputs["L0_attn_x"])
    cv = np.asarray(inputs["L0_c"])
    in_maps = []
    for c in range(B):
        m = dict(shared)
        m["x"] = np.ascontiguousarray(x[c].T)
        m["cvec"] = np.ascontiguousarray(cv[c].reshape(DC, 128).T)
        in_maps.append(m)
    nc = build_nc(S, NH, IDIM, E)
    resu = run_bass_kernel_spmd(nc, in_maps, core_ids=list(range(B)))
    out = np.stack([r["y"].T for r in resu.results])
    return np.ascontiguousarray(out.astype(np.float32))


# version 34
# speedup vs baseline: 1.0047x; 1.0047x over previous
import numpy as np
import ml_dtypes
from contextlib import ExitStack

import concourse.bass as bass
import concourse.mybir as mybir
from concourse.bass_utils import run_bass_kernel_spmd

F32 = mybir.dt.float32
BF16 = mybir.dt.bfloat16
AF = mybir.ActivationFunctionType
ALU = mybir.AluOpType
AX = mybir.AxisListType

NEG = -1.0e9
LN_EPS = 1e-5
D = 1024
DC = D // 128

ENGINES = ("tensor", "vector", "scalar", "sync", "gpsimd")
OWN_SEM = {"tensor": "st", "vector": "sv", "scalar": "ss", "gpsimd": "sg"}


class Res:
    __slots__ = ("w", "r")

    def __init__(self):
        self.w = None
        self.r = []


class Ev:
    __slots__ = ("sem", "seq", "used", "val")

    def __init__(self, sem, seq):
        self.sem = sem
        self.seq = seq
        self.used = False
        self.val = None


class Emit:

    def __init__(self, nc):
        self.nc = nc
        self.prog = []
        self.seqs = {}
        self.sems = {}
        self.ctx = ExitStack()

    def sem(self, name):
        if name not in self.sems:
            self.sems[name] = self.nc.semaphore(name).__enter__()
            self.seqs[name] = 0

    def op(self, eng, fn, reads=(), writes=(), sem=None, delta=1):
        if sem is None:
            sem = OWN_SEM[eng]
        self.sem(sem)
        waits = {}

        def add(ev):
            if ev is None:
                return
            k = ev.sem
            if k not in waits or waits[k].seq < ev.seq:
                waits[k] = ev

        for r in reads:
            add(r.w)
        for w in writes:
            add(w.w)
            for ev in w.r:
                add(ev)
        self.seqs[sem] += 1
        ev = Ev(sem, self.seqs[sem])
        if delta != 1:
            ev.used = True
        self.prog.append((eng, fn, list(waits.values()), ev, delta))
        for w in writes:
            w.w = ev
            w.r = []
        for r in reads:
            r.r.append(ev)
        return ev

    def raw(self, eng, fn):
        self.prog.append((eng, fn, [], None, 0))

    def flush(self):
        self.prog.append(("BARRIER", None, [], None, 0))

    def wait_event(self, eng, ev):
        self.prog.append((eng, None, [ev], None, 0))

    def finalize(self):
        for (_e, _f, waits, _ev, _d) in self.prog:
            for ev in waits:
                ev.used = True
        cum = {k: 0 for k in self.seqs}
        for (_e, _f, _w, ev, delta) in self.prog:
            if ev is not None and ev.used:
                cum[ev.sem] += delta
                ev.val = cum[ev.sem]
        sems = self.sems
        blocks = []
        cur = []
        for entry in self.prog:
            if entry[0] == "BARRIER":
                blocks.append(cur)
                cur = []
            else:
                cur.append(entry)
        if cur:
            blocks.append(cur)
        hw = {}
        for blk_ops in blocks:
            per_eng = {}
            for (eng, fn, waits, ev, delta) in blk_ops:
                final = []
                h = hw.setdefault(eng, {})
                for wv in waits:
                    assert wv.val is not None
                    if h.get(wv.sem, 0) < wv.val:
                        final.append((wv.sem, wv.val))
                        h[wv.sem] = wv.val
                inc = (ev.sem, ev.val, delta) if (
                    ev is not None and ev.used) else None
                per_eng.setdefault(eng, []).append((fn, final, inc))
            with self.nc.Block() as blk:
                for eng, ops in per_eng.items():
                    def body(E, ops=tuple(ops)):
                        for (fn, final, inc) in ops:
                            for (sname, v) in final:
                                E.wait_ge(sems[sname], v)
                            if fn is None:
                                continue
                            ins = fn(E)
                            if inc is not None:
                                ins.then_inc(sems[inc[0]], inc[2])
                    getattr(blk, eng)(body)


def build_nc(S=2048, NH=16, IDIM=2048, E=8, CAP=384, debug_stop=None):
    FC = NH // 2
    NB = S // 128
    NTC = S // 512
    IC = IDIM // 128
    IH = IDIM // 2
    NSC = CAP // 128
    assert S % 512 == 0 and NH % 2 == 0 and IDIM % 256 == 0

    nc = bass.Bass("TRN2", target_bir_lowering=False)

    def din(name, shape, dtype=F32):
        return nc.dram_tensor(name, shape, dtype, kind="ExternalInput")

    x_h = din("x", [D, S])
    c_h = din("cvec", [128, DC])
    wmod_h = din("wmod", [D, 6 * D])
    wq2_h = din("wq2", [FC, DC, 128, 128])
    wk2_h = din("wk2", [FC, DC, 128, 128])
    wv2_h = din("wv2", [FC, DC, 128, 128])
    wo_h = din("wo", [FC * DC, 128, 128])
    gw_h = din("gw", [DC, 128, E])
    shg_h = din("shg", [2, DC, 128, IH])
    shu_h = din("shu", [2, DC, 128, IH])
    shd_h = din("shd", [IDIM, D])
    rtg_h = din("rtg", [E, 2, DC, 128, IH])
    rtu_h = din("rtu", [E, 2, DC, 128, IH])
    rtd_h = din("rtd", [E, IDIM, D])
    diagm_h = din("diagm", [128, 128])
    ones128_h = din("ones128", [1, 128])
    nones128_h = din("nones128", [1, 128])
    onescol_h = din("onescol", [128, 1], BF16)
    eab_h = din("eab", [1, 256], BF16)
    ones128b_h = din("ones128b", [1, 128], BF16)
    identb_h = din("identb", [128, 128], BF16)
    identf_h = din("identf", [128, 128])
    w8_h = din("w8", [128, E])
    ufull_h = din("ufull", [128, 128], BF16)
    ustrict_h = din("ustrict", [128, 128], BF16)
    iotaf_h = din("iotaf", [128, 512])
    iotasc_h = din("iotasc", [128, CAP // 128])
    identh_h = din("identh", [128, 128], mybir.dt.float16)
    onesh_h = din("onesh", [1, 128], mybir.dt.float16)
    e65_h = din("e65", [68, 256], BF16)
    y_h = nc.dram_tensor("y", [D, S], F32, kind="ExternalOutput")

    em = Emit(nc)

    with em.ctx:
        ctx = em.ctx

        def sbuf(name, shape, dtype=F32, stack=None):
            return (stack or ctx).enter_context(
                nc.sbuf_tensor(name, shape, dtype))

        def psum(name, shape, stack=None):
            return (stack or ctx).enter_context(
                nc.psum_tensor(name, shape, F32))

        def dma(dst_ap, src_ap, writes, reads=(), eng="sync", sem=None):
            assert sem is not None
            return em.op(eng,
                         lambda E, d=dst_ap, s=src_ap: E.dma_start(d, s),
                         reads=reads, writes=writes, sem=sem, delta=16)

        def mmg(chain, writes, reads):
            n = len(chain)
            for i, (o, l, r) in enumerate(chain):
                em.op("tensor",
                      lambda E, o=o, l=l, r=r, a=(i == 0), z=(i == n - 1):
                      E.matmul(o, l, r, start=a, stop=z,
                               skip_group_check=True),
                      reads=(reads if (i == 0 or i == n - 1) else ()),
                      writes=(writes if (i == 0 or i == n - 1) else ()))

        Xf = [sbuf(f"Xf{i}", [128, S]) for i in range(DC)]
        Hb = [sbuf(f"Hb{i}", [128, S], BF16) for i in range(DC)]
        rXf = [Res() for _ in range(DC)]
        rHb = [Res() for _ in range(DC)]
        mod_sb = sbuf("mod_sb", [128, 6 * DC]); rmod = Res()
        c_sb = sbuf("c_sb", [128, DC]); rc = Res()
        c_sil = sbuf("c_sil", [128, DC], F32); rcs = Res()
        diagm = sbuf("diagm_s", [128, 128])
        ones128 = sbuf("ones128_s", [1, 128])
        nones128 = sbuf("nones128_s", [1, 128])
        onescol = sbuf("onescol_s", [128, 1], BF16)
        eab = sbuf("eab_s", [1, 256], BF16)
        ones128b = sbuf("ones128b_s", [1, 128], BF16)
        identb = sbuf("identb_s", [128, 128], BF16)
        identf = sbuf("identf_s", [128, 128])
        w8t = sbuf("w8t", [128, E])
        ufull = sbuf("ufull_s", [128, 128], BF16)
        ustrict = sbuf("ustrict_s", [128, 128], BF16)
        iotaf = sbuf("iotaf_s", [128, 512])
        iotasc = sbuf("iotasc_s", [128, CAP // 128])
        identh = sbuf("identh_s", [128, 128], mybir.dt.float16)
        onesh = sbuf("onesh_s", [1, 128], mybir.dt.float16)
        e65 = sbuf("e65_s", [68, 256], BF16)
        rconst = Res()
        sgn_t = sbuf("sgn_t", [128, DC]); rsgn = Res()
        oh_bf = sbuf("oh_bf", [128, NB * E], BF16); rohb = Res()
        oh_f = sbuf("oh_f", [128, NB * E]); rohf = Res()
        score_f = sbuf("score_f", [128, NB]); rscf = Res()
        rank_f = sbuf("rank_f", [128, NB * E]); rrk = Res()

        for i in range(DC):
            _last_sx = dma(Xf[i][:], x_h.ap()[i * 128:(i + 1) * 128, :],
                           writes=[rXf[i]], sem="sx")
        _last_sx = dma(c_sb[:], c_h.ap(), writes=[rc], sem="sx")
        for (t, h) in ((diagm, diagm_h), (ones128, ones128_h),
                       (nones128, nones128_h), (onescol, onescol_h),
                       (eab, eab_h), (ones128b, ones128b_h),
                       (identb, identb_h), (identf, identf_h), (w8t, w8_h),
                       (ufull, ufull_h), (ustrict, ustrict_h),
                       (iotaf, iotaf_h), (iotasc, iotasc_h),
                       (identh, identh_h), (onesh, onesh_h),
                       (e65, e65_h)):
            _last_sx = dma(t[:], h.ap(), writes=[rconst], sem="sx")
        for _r in [rc, rconst] + rXf:
            _r.w = _last_sx

        em.op("scalar", lambda E: E.activation(c_sil[:], c_sb[:], AF.Silu),
              reads=[rc], writes=[rcs])

        CA, GA, SB_, CB, GB = DC, 2 * DC, 3 * DC, 4 * DC, 5 * DC

        with ExitStack() as ph0:
            wmod_bf = [sbuf(f"wmodb{i}", [128, 6 * D], F32, ph0)
                       for i in range(2)]
            rwm = [[Res(), Res()], [Res(), Res()]]
            mod_ps = psum("mod_ps", [128, 6 * DC], ph0)
            rmodps = Res()
            NJ = 6 * DC
            for dc in range(DC):
                b = dc % 2
                dma(wmod_bf[b][:, 0:3 * D],
                    wmod_h.ap()[dc * 128:(dc + 1) * 128, 0:3 * D],
                    writes=[rwm[b][0]], eng="gpsimd", sem=f"dwm{b}")
                dma(wmod_bf[b][:, 3 * D:6 * D],
                    wmod_h.ap()[dc * 128:(dc + 1) * 128, 3 * D:6 * D],
                    writes=[rwm[b][1]], eng="scalar", sem=f"dwn{b}")
                for j in range(NJ):
                    if j == 0:
                        rd = [rwm[b][0], rcs]
                    elif j == NJ // 2:
                        rd = [rwm[b][1]]
                    elif j == NJ - 1:
                        rd = [rwm[b][0], rwm[b][1]]
                    else:
                        rd = []
                    em.op("tensor",
                          lambda E, b=b, j=j, dc=dc: E.matmul(
                              mod_ps[:, j:j + 1],
                              wmod_bf[b][:, j * 128:(j + 1) * 128],
                              c_sil[:, dc:dc + 1],
                              start=(dc == 0 and j == 0),
                              stop=(dc == DC - 1 and j == NJ - 1),
                              skip_group_check=True),
                          reads=rd,
                          writes=([rmodps] if (j in (0, NJ - 1)) and
                                  (dc == 0 or dc == DC - 1) else ()))
            em.op("scalar", lambda E: E.copy(mod_sb[:], mod_ps[:]),
                  reads=[rmodps], writes=[rmod])

            em.op("vector",
                  lambda E: E.tensor_scalar_add(mod_sb[:, CA:CA + DC],
                                                mod_sb[:, CA:CA + DC], 1.0),
                  reads=[rmod], writes=[rmod])
            em.op("vector",
                  lambda E: E.tensor_scalar_add(mod_sb[:, CB:CB + DC],
                                                mod_sb[:, CB:CB + DC], 1.0),
                  reads=[rmod], writes=[rmod])
            for G in (GA, GB):
                em.op("vector",
                      lambda E, G=G: E.tensor_scalar(
                          sgn_t[:], mod_sb[:, G:G + DC], 0.0, 2.0,
                          ALU.is_ge, ALU.mult),
                      reads=[rmod], writes=[rsgn])
                em.op("vector",
                      lambda E: E.tensor_scalar_add(sgn_t[:], sgn_t[:], -1.0),
                      reads=[rsgn], writes=[rsgn])
                em.op("scalar",
                      lambda E, G=G: E.activation(
                          mod_sb[:, G:G + DC], mod_sb[:, G:G + DC], AF.Abs),
                      reads=[rmod], writes=[rmod])
                em.op("vector",
                      lambda E, G=G: E.tensor_scalar_max(
                          mod_sb[:, G:G + DC], mod_sb[:, G:G + DC], 0.25),
                      reads=[rmod], writes=[rmod])
                em.op("vector",
                      lambda E, G=G: E.tensor_mul(
                          mod_sb[:, G:G + DC], mod_sb[:, G:G + DC],
                          sgn_t[:]),
                      reads=[rmod, rsgn], writes=[rmod])
            em.flush()

        def emit_ln(cG, sG, tag, post_tc=None, do_flush=True):
            with ExitStack() as lnx:
                rstd_row = sbuf(f"rstd_row{tag}", [1, S], F32, lnx)
                rrstd = Res()
                negm_row = sbuf(f"negm_row{tag}", [1, S], F32, lnx)
                rnegm = Res()
                sq_buf = [sbuf(f"sqb{tag}{i}", [128, 512], BF16, lnx)
                          for i in range(2)]
                rsq = [Res(), Res()]
                app_t = sbuf(f"app_t{tag}", [128, 512], F32, lnx)
                rapp = Res()
                stat_ps = psum(f"stat_ps{tag}", [1, 512], lnx); r1 = Res()
                stat2_ps = psum(f"stat2_ps{tag}", [1, 512], lnx); r2 = Res()
                bcA_ps = psum(f"bcA_ps{tag}", [128, 512], lnx); rbA = Res()
                bcB_ps = psum(f"bcB_ps{tag}", [128, 512], lnx); rbB = Res()
                for dc in range(DC):
                    em.op("vector",
                          lambda E, d=dc: E.tensor_copy(Hb[d][:], Xf[d][:]),
                          reads=[rXf[dc]], writes=[rHb[dc]])
                for tc in range(NTC):
                    sl = slice(tc * 512, (tc + 1) * 512)
                    for dc in range(DC):
                        em.op("scalar",
                              lambda E, d=dc, s=sl: E.activation(
                                  sq_buf[d % 2][:], Hb[d][:, s], AF.Square),
                              reads=[rHb[dc]], writes=[rsq[dc % 2]])
                        em.op("tensor",
                              lambda E, d=dc, s=sl: E.matmul(
                                  stat_ps[:], onescol[:], Hb[d][:, s],
                                  start=(d == 0), stop=(d == DC - 1),
                                  skip_group_check=True),
                              reads=[rHb[dc], rconst],
                              writes=([r1] if dc in (0, DC - 1) else ()))
                        em.op("tensor",
                              lambda E, d=dc: E.matmul(
                                  stat2_ps[:], onescol[:], sq_buf[d % 2][:],
                                  start=(d == 0), stop=(d == DC - 1),
                                  skip_group_check=True),
                              reads=[rsq[dc % 2]],
                              writes=([r2] if dc in (0, DC - 1) else ()))
                    em.op("scalar",
                          lambda E, s=sl: E.copy(negm_row[:, s], stat_ps[:]),
                          reads=[r1], writes=[rnegm])
                    em.op("scalar",
                          lambda E, s=sl: E.copy(rstd_row[:, s], stat2_ps[:]),
                          reads=[r2], writes=[rrstd])

                    em.op("vector",
                          lambda E, s=sl: E.tensor_scalar_mul(
                              negm_row[:, s], negm_row[:, s], 1.0 / D),
                          reads=[rnegm], writes=[rnegm])
                    em.op("vector",
                          lambda E, s=sl: E.tensor_scalar_mul(
                              rstd_row[:, s], rstd_row[:, s], 1.0 / D),
                          reads=[rrstd], writes=[rrstd])
                    em.op("vector",
                          lambda E, s=sl: E.tensor_mul(
                              stat_ps[:], negm_row[:, s], negm_row[:, s]),
                          reads=[rnegm], writes=[r1])
                    em.op("vector",
                          lambda E, s=sl: E.tensor_sub(
                              rstd_row[:, s], rstd_row[:, s], stat_ps[:]),
                          reads=[rrstd, r1], writes=[rrstd])
                em.op("vector",
                      lambda E: E.tensor_scalar_add(rstd_row[:], rstd_row[:],
                                                    LN_EPS),
                      reads=[rrstd], writes=[rrstd])
                em.op("scalar",
                      lambda E: E.activation(rstd_row[:], rstd_row[:], AF.Ln),
                      reads=[rrstd], writes=[rrstd])
                em.op("vector",
                      lambda E: E.tensor_scalar_mul(rstd_row[:], rstd_row[:],
                                                    -0.5),
                      reads=[rrstd], writes=[rrstd])
                em.op("scalar",
                      lambda E: E.activation(rstd_row[:], rstd_row[:], AF.Exp),
                      reads=[rrstd], writes=[rrstd])
                em.op("vector",
                      lambda E: E.tensor_mul(negm_row[:], negm_row[:],
                                             rstd_row[:]),
                      reads=[rrstd, rnegm], writes=[rnegm])
                for tc in range(NTC):
                    sl = slice(tc * 512, (tc + 1) * 512)
                    em.op("tensor",
                          lambda E, s=sl: E.matmul(
                              bcA_ps[:], ones128[:], rstd_row[:, s],
                              start=True, stop=True, skip_group_check=True),
                          reads=[rrstd, rconst], writes=[rbA])
                    em.op("tensor",
                          lambda E, s=sl: E.matmul(
                              bcB_ps[:], nones128[:], negm_row[:, s],
                              start=True, stop=True, skip_group_check=True),
                          reads=[rnegm, rconst], writes=[rbB])
                    for dc in range(DC):
                        em.op("vector",
                              lambda E, d=dc, s=sl: E.tensor_mul(
                                  app_t[:], Xf[d][:, s], bcA_ps[:]),
                              reads=[rXf[dc], rbA], writes=[rapp])
                        em.op("vector",
                              lambda E: E.tensor_add(app_t[:], app_t[:],
                                                     bcB_ps[:]),
                              reads=[rbB, rapp], writes=[rapp])
                        em.op("scalar",
                              lambda E, d=dc, s=sl, cg=cG, sg=sG:
                              E.activation(
                                  Hb[d][:, s], app_t[:], AF.Identity,
                                  bias=mod_sb[:, sg + d:sg + d + 1],
                                  scale=mod_sb[:, cg + d:cg + d + 1]),
                              reads=[rapp, rmod], writes=[rHb[dc]])
                    if post_tc is not None:
                        post_tc(tc)
                if do_flush:
                    em.flush()

        def _dump_and_exit(aps_res):
            rout = Res()
            evs = []
            for (dst, src, rr, eng) in aps_res:
                evs.append(dma(dst, src, writes=[rout], reads=[rr],
                               eng=eng, sem="so"))
            for ev in evs:
                em.wait_event("sync", ev)
            em.flush()
            em.finalize()

        if debug_stop == "mod":
            _dump_and_exit([(y_h.ap()[0:128, 0:6 * DC], mod_sb[:], rmod,
                             "sync")])
            return nc

        ph2 = ExitStack()
        if True:
            wq2b = [sbuf(f"wq2b{i}", [128, DC * 128], BF16, ph2)
                    for i in range(2)]
            wk2b = [sbuf(f"wk2b{i}", [128, DC * 128], BF16, ph2)
                    for i in range(2)]
            wv2b = [sbuf(f"wv2b{i}", [128, DC * 128], BF16, ph2)
                    for i in range(2)]
            rwq = [Res(), Res()]
            rwk = [Res(), Res()]
            rwv = [Res(), Res()]
            z_flat = [sbuf(f"zfl{i}", [128, S], BF16, ph2) for i in range(FC)]
            rz = [Res() for _ in range(FC)]
            Qall = [sbuf(f"Qall{i}", [128, S], BF16, ph2) for i in range(2)]
            Kall = [sbuf(f"Kall{i}", [128, S], BF16, ph2) for i in range(2)]
            Vb = [sbuf(f"Vb{i}", [128, NB * 130], BF16, ph2)
                  for i in range(2)]
            rQ = [Res(), Res()]
            rK = [Res(), Res()]
            rV = [Res(), Res()]
            PT = [sbuf(f"PT{i}", [128, 1024], BF16, ph2) for i in range(2)]
            rPT = [Res(), Res()]
            rcp_sb = sbuf("rcp_sb", [68, 4608], BF16, ph2); rrF = Res()
            rrs = [[Res(), Res()] for _ in range(NTC)]
            nrm_bf = sbuf("nrm_bf", [128, 512], BF16, ph2); rnb = Res()

            ut_ps = [psum(f"ut_ps{i}", [128, 512], ph2) for i in range(2)]
            rut = [Res(), Res()]

            def load_attn_w(fc):
                b = fc % 2
                for (nm, dst, src, rr) in (("q", wq2b[b], wq2_h, rwq[b]),
                                           ("k", wk2b[b], wk2_h, rwk[b]),
                                           ("v", wv2b[b], wv2_h, rwv[b])):
                    dma(dst[:].rearrange("p (c n) -> p c n", c=DC),
                        src.ap()[fc].rearrange("c p n -> p c n"),
                        writes=[rr], eng="gpsimd", sem=f"dw{nm}{b}")

            load_attn_w(0)
            if FC > 1:
                load_attn_w(1)

            uctr = [0]

            def proj_units(fcn):
                par = fcn % 2
                units = []
                for (dstT, rdst, wsrc, rw, scl) in (
                        (Qall[par], rQ[par], wq2b[par], rwq[par], 0.125),
                        (Kall[par], rK[par], wk2b[par], rwk[par], 1.0)):
                    for tc in range(NTC):
                        def u(dstT=dstT, rdst=rdst, wsrc=wsrc, rw=rw,
                              scl=scl, tc=tc):
                            sl = slice(tc * 512, (tc + 1) * 512)
                            ub = uctr[0] % 2
                            uctr[0] += 1
                            mmg([(ut_ps[ub][:],
                                  wsrc[:, dc * 128:(dc + 1) * 128],
                                  Hb[dc][:, sl]) for dc in range(DC)],
                                writes=[rut[ub]], reads=[rw] + rHb)
                            em.op("vector",
                                  lambda E, d=dstT, s=sl, sc=scl, ub=ub:
                                  E.tensor_scalar_mul(d[:, s], ut_ps[ub][:],
                                                      sc),
                                  reads=[rut[ub]], writes=[rdst])
                        units.append(u)
                for tb in range(NB):
                    def u(tb=tb, par=par):
                        ub = uctr[0] % 2
                        uctr[0] += 1
                        mmg([(ut_ps[ub][:, 0:128],
                              Hb[dc][:, tb * 128:(tb + 1) * 128],
                              wv2b[par][:, dc * 128:(dc + 1) * 128])
                             for dc in range(DC)],
                            writes=[rut[ub]], reads=[rwv[par]] + rHb)
                        em.op("vector",
                              lambda E, t=tb, ub=ub, par=par: E.tensor_copy(
                                  Vb[par][:, t * 130:t * 130 + 64],
                                  ut_ps[ub][:, 0:64]),
                              reads=[rut[ub]], writes=[rV[par]])
                        em.op("vector",
                              lambda E, t=tb, ub=ub, par=par: E.tensor_copy(
                                  Vb[par][:, t * 130 + 65:t * 130 + 129],
                                  ut_ps[ub][:, 64:128]),
                              reads=[rut[ub]], writes=[rV[par]])
                    units.append(u)
                return units

            for par in range(2):
                em.op("vector",
                      lambda E, par=par: E.memset(Vb[par][:], 1.0),
                      writes=[rV[par]])

            p0units = proj_units(0)
            p0i = [0]

            def ln1_post(tc):
                n = len(p0units) * (tc + 1) // NTC
                while p0i[0] < n:
                    p0units[p0i[0]]()
                    p0i[0] += 1

            emit_ln(CA, 0, "a", post_tc=ln1_post)

            if debug_stop == "ln1":
                _dump_and_exit([(y_h.ap()[dc * 128:(dc + 1) * 128, :],
                                 Hb[dc][:], rHb[dc], "gpsimd")
                                for dc in range(DC)])
                return nc

            s2_ps = [psum(f"s2_ps{i}", [128, 1024], ph2)
                     for i in range(2)]
            rs2 = [Res(), Res()]
            z2_ps = [psum(f"z2_ps{i}", [128, 512], ph2) for i in range(2)]
            rz2 = [Res(), Res()]
            zstage = sbuf("zstage", [64, 512], BF16, ph2); rzs = Res()
            lsb = sbuf("lsb", [68, 512], F32, ph2); rlw = Res()
            HT = FC * DC // 2
            wob = [sbuf(f"wob{i}", [128, HT * 128], BF16, ph2)
                   for i in range(2)]
            rwo = [Res(), Res()]
            for i in range(2):
                dma(wob[i][:].rearrange("p (c n) -> p c n", c=HT),
                    wo_h.ap()[i * HT:(i + 1) * HT]
                    .rearrange("c p n -> p c n"),
                    writes=[rwo[i]], eng="gpsimd", sem="dwo")
            while p0i[0] < len(p0units):
                p0units[p0i[0]]()
                p0i[0] += 1

            for fc in range(FC):
                b = fc % 2
                if fc + 2 < FC:
                    load_attn_w(fc + 2)
                units = proj_units(fc + 1) if fc + 1 < FC else []
                ui = 0
                ngr_total = 2 * sum(2 * qc + 2 for qc in range(NTC))
                gctr = 0
                for hh in range(2):
                    pr = slice(hh * 64, (hh + 1) * 64)
                    zr = slice(hh * 64, hh * 64 + 64)
                    for qc in range(NTC):
                        qsl = slice(qc * 512, (qc + 1) * 512)
                        npc = 4 * qc + 4
                        groups = [(g, min(g + 2, npc))
                                  for g in range(0, npc, 2)]

                        def emit_S(gi):
                            p0, p1 = groups[gi]
                            sb_, rsb = s2_ps[gi % 2], rs2[gi % 2]
                            pt, rpt = PT[gi % 2], rPT[gi % 2]
                            lo = {}
                            for pc in range(p0, p1):
                                h0 = (pc - p0) * 512
                                col_lo = max(0, pc * 128 - qc * 512)
                                lo[pc] = col_lo
                                cs = slice(qc * 512 + col_lo, (qc + 1) * 512)
                                em.op("tensor",
                                      lambda E, p=sb_, c=col_lo, k=pc, q=cs,
                                      h=pr, o=h0, bb=b:
                                      E.matmul(
                                          p[:, o + c:o + 512],
                                          Kall[bb][h, k * 128:(k + 1) * 128],
                                          Qall[bb][h, q], start=True,
                                          stop=True, skip_group_check=True),
                                      reads=[rK[b], rQ[b]], writes=[rsb])
                                if pc * 128 >= qc * 512:
                                    em.op("vector",
                                          lambda E, p=sb_, c=col_lo, o=h0:
                                          E.tensor_add(
                                              p[:, o + c:o + c + 128],
                                              p[:, o + c:o + c + 128],
                                              diagm[:]),
                                          reads=[rconst], writes=[rsb])
                            if (p1 - p0 == 2 and lo[p0] == 0
                                    and lo[p0 + 1] == 0):
                                em.op("scalar",
                                      lambda E, p=sb_, t=pt: E.activation(
                                          t[:, 0:1024], p[:, 0:1024],
                                          AF.Exp),
                                      reads=[rsb], writes=[rpt])
                            else:
                                for pc in range(p0, p1):
                                    h0 = (pc - p0) * 512
                                    c = lo[pc]
                                    em.op("scalar",
                                          lambda E, p=sb_, t=pt, c=c, o=h0:
                                          E.activation(
                                              t[:, o + c:o + 512],
                                              p[:, o + c:o + 512], AF.Exp),
                                          reads=[rsb], writes=[rpt])
                            return lo

                        def emit_PV(gi, lo):
                            p0, p1 = groups[gi]
                            pt, rpt = PT[gi % 2], rPT[gi % 2]
                            for pc in range(p0, p1):
                                h0 = (pc - p0) * 512
                                c = lo[pc]
                                em.op("tensor",
                                      lambda E, c=c, k=pc, t=pt, o=h0,
                                      a=(pc == 0), z=(pc == npc - 1),
                                      h=hh, bb=b:
                                      E.matmul(
                                          z2_ps[h][0:65, c:512],
                                          Vb[bb][:, k * 130 + h * 65:
                                                 k * 130 + h * 65 + 65],
                                          t[:, o + c:o + 512], start=a,
                                          stop=z, skip_group_check=True),
                                      reads=[rpt, rV[b]],
                                      writes=([rz2[hh]]
                                              if pc in (0, npc - 1)
                                              else ()))

                        prev = None
                        for gi in range(len(groups)):
                            lo = emit_S(gi)
                            if prev is not None:
                                emit_PV(prev[0], prev[1])
                            prev = (gi, lo)
                            gctr += 1
                            while (units and
                                   ui < len(units) * gctr // ngr_total):
                                units[ui]()
                                ui += 1
                        emit_PV(prev[0], prev[1])
                        if hh == 0:
                            em.op("vector",
                                  lambda E, f=fc, q=qsl: E.tensor_copy(
                                      z_flat[f][0:64, q],
                                      z2_ps[0][0:64, :]),
                                  reads=[rz2[0]], writes=[rz[fc]])
                        else:
                            em.op("vector",
                                  lambda E: E.tensor_copy(
                                      zstage[:], z2_ps[1][0:64, :]),
                                  reads=[rz2[1]], writes=[rzs])
                            dma(z_flat[fc][64:128, qsl], zstage[:],
                                writes=[rz[fc]], reads=[rzs], sem="dzs")
                        em.op("vector",
                              lambda E, h=hh: E.tensor_copy(
                                  lsb[64:65, :], z2_ps[h][64:65, :]),
                              reads=[rz2[hh]], writes=[rlw])
                        if qc != NTC - 1:
                            dma(lsb[65 + qc:66 + qc, :], lsb[64:65, :],
                                writes=[rlw], reads=[rlw], sem="dls")
                    em.op("scalar",
                          lambda E: E.activation(
                              lsb[64:64 + NTC, :], lsb[64:64 + NTC, :],
                              AF.Ln),
                          reads=[rlw], writes=[rlw])
                    em.op("vector",
                          lambda E: E.tensor_scalar_mul(
                              lsb[64:64 + NTC, :], lsb[64:64 + NTC, :],
                              -1.0),
                          reads=[rlw], writes=[rlw])
                    em.op("scalar",
                          lambda E: E.activation(
                              rcp_sb[64:64 + NTC, 4096:4608],
                              lsb[64:64 + NTC, :], AF.Exp),
                          reads=[rlw], writes=[rrF])
                    for t2 in range(NTC):
                        sr = 64 if t2 == NTC - 1 else 65 + t2
                        dma(rcp_sb[64:65, t2 * 1024 + hh * 512:
                                   t2 * 1024 + hh * 512 + 512],
                            rcp_sb[sr:sr + 1, 4096:4608],
                            writes=[rrs[t2][hh]], reads=[rrF],
                            sem=f"dlo{t2}{hh}")
                while ui < len(units):
                    units[ui]()
                    ui += 1
                for tc in range(NTC):
                    sl = slice(tc * 512, (tc + 1) * 512)
                    ub = uctr[0] % 2
                    uctr[0] += 1
                    em.op("tensor",
                          lambda E, t=tc, ub=ub: E.matmul(
                              ut_ps[ub][:], e65[64:65, 0:128],
                              rcp_sb[64:65, t * 1024:t * 1024 + 512],
                              start=True, stop=False, skip_group_check=True),
                          reads=[rrs[tc][0], rconst], writes=[rut[ub]])
                    em.op("tensor",
                          lambda E, t=tc, ub=ub: E.matmul(
                              ut_ps[ub][:], e65[64:65, 128:256],
                              rcp_sb[64:65, t * 1024 + 512:
                                     (t + 1) * 1024],
                              start=False, stop=True, skip_group_check=True),
                          reads=[rrs[tc][1], rconst], writes=[rut[ub]])
                    em.op("scalar",
                          lambda E, ub=ub: E.copy(nrm_bf[:], ut_ps[ub][:]),
                          reads=[rut[ub]], writes=[rnb])
                    em.op("vector",
                          lambda E, f=fc, s=sl: E.tensor_mul(
                              z_flat[f][:, s], z_flat[f][:, s], nrm_bf[:]),
                          reads=[rnb, rz[fc]], writes=[rz[fc]])

            for tc in range(NTC):
                sl = slice(tc * 512, (tc + 1) * 512)
                for dc in range(DC):
                    ub = uctr[0] % 2
                    uctr[0] += 1
                    mmg([(ut_ps[ub][:],
                          wob[(fc * DC + dc) // HT]
                          [:, ((fc * DC + dc) % HT) * 128:
                           ((fc * DC + dc) % HT + 1) * 128],
                          z_flat[fc][:, sl]) for fc in range(FC)],
                        writes=[rut[ub]], reads=rwo + rz)
                    em.op("vector",
                          lambda E, d=dc, s=sl, ub=ub:
                          E.scalar_tensor_tensor(
                              Xf[d][:, s], ut_ps[ub][:],
                              mod_sb[:, GA + d:GA + d + 1], Xf[d][:, s],
                              ALU.mult, ALU.add),
                          reads=[rut[ub], rmod, rXf[dc]], writes=[rXf[dc]])
            em.flush()
            ph2.close()

        if debug_stop == "attn":
            _dump_and_exit([(y_h.ap()[dc * 128:(dc + 1) * 128, :], Xf[dc][:],
                             rXf[dc], "sync") for dc in range(DC)])
            return nc

        with ExitStack() as ph4:
            NSC = CAP // 128
            gw_bf = sbuf("gw_bf", [128, DC * E], BF16, ph4); rgw = Res()
            mx_sb = sbuf("mx_sb", [128, 1], F32, ph4); rmx = Res()
            sc_sb = sbuf("sc_sb", [128, 1], F32, ph4); rsc = Res()
            eq_sb = sbuf("eq_sb", [128, E], F32, ph4); req = Res()
            m2_sb = sbuf("m2_sb", [128, 1], F32, ph4); rm2 = Res()
            rank16 = sbuf("rank16", [128, NB * E], mybir.dt.float16, ph4)
            rrk16 = Res()
            rank_row = sbuf("rank_row", [1, 512], mybir.dt.float16, ph4)
            rrow = Res()

            dma(gw_bf[:].rearrange("p (c n) -> p c n", c=DC),
                gw_h.ap().rearrange("c p n -> p c n"),
                writes=[rgw], eng="gpsimd", sem="dgw")

            lnps = ExitStack()
            lg_ps = lnps.enter_context(nc.psum_tensor("lg_ps", [128, E], F32))
            rlg = Res()

            def router_tc(tc):
                for t4 in range(4):
                    tb = tc * 4 + t4
                    if tb >= NB:
                        break
                    mmg([(lg_ps[:, 0:E], Hb[dc][:, tb * 128:(tb + 1) * 128],
                          gw_bf[:, dc * E:(dc + 1) * E])
                         for dc in range(DC)],
                        writes=[rlg], reads=[rgw] + rHb)
                    em.op("vector",
                          lambda E_, NE=E: E_.reduce_max(
                              mx_sb[:], lg_ps[:, 0:NE], AX.X),
                          reads=[rlg], writes=[rmx])
                    em.op("vector",
                          lambda E_, NE=E: E_.tensor_scalar(
                              eq_sb[:], lg_ps[:, 0:NE], mx_sb[:], None,
                              ALU.is_equal),
                          reads=[rlg, rmx], writes=[req])
                    em.op("vector",
                          lambda E: E.tensor_mul(eq_sb[:], eq_sb[:], w8t[:]),
                          reads=[req, rconst], writes=[req])
                    em.op("vector",
                          lambda E: E.reduce_max(m2_sb[:], eq_sb[:], AX.X),
                          reads=[req], writes=[rm2])
                    em.op("vector",
                          lambda E_, t=tb, NE=E: E_.tensor_scalar(
                              oh_bf[:, t * NE:(t + 1) * NE], eq_sb[:],
                              m2_sb[:], None, ALU.is_equal),
                          reads=[req, rm2], writes=[rohb])
                    em.op("vector",
                          lambda E_, t=tb, NE=E: E_.tensor_scalar(
                              oh_f[:, t * NE:(t + 1) * NE], eq_sb[:],
                              m2_sb[:], None, ALU.is_equal),
                          reads=[req, rm2], writes=[rohf])
                    em.op("scalar",
                          lambda E: E.activation(sc_sb[:], mx_sb[:],
                                                 AF.Sigmoid),
                          reads=[rmx], writes=[rsc])
                    em.op("vector",
                          lambda E_, t=tb: E_.tensor_copy(
                              score_f[:, t:t + 1], sc_sb[:]),
                          reads=[rsc], writes=[rscf])

            emit_ln(CB, SB_, "b", post_tc=router_tc, do_flush=False)
            em.flush()
            lnps.close()
            NGB = 3
            wg_ch = [sbuf(f"wg_ch{i}", [128, DC * 128], BF16, ph4)
                     for i in range(NGB)]
            wu_ch = [sbuf(f"wu_ch{i}", [128, DC * 128], BF16, ph4)
                     for i in range(NGB)]
            rwg = [Res() for _ in range(NGB)]
            rwu = [Res() for _ in range(NGB)]
            wd_bf = [sbuf(f"wd_bf{i}", [128, D], BF16, ph4)
                     for i in range(2)]
            rwd = [Res() for _ in range(2)]
            act_sl = sbuf("act_sl", [128, IC * 512], BF16, ph4); ract = Res()
            silu_sb = [sbuf("silu0", [128, 512], F32, ph4)] * 2
            rsilu0 = Res()
            rsilu = [rsilu0, rsilu0]
            pp = [psum(f"pp{i}", [128, 512], ph4) for i in range(8)]
            rpp = [Res() for _ in range(8)]

            if debug_stop == "ln2":
                em.flush()
                _dump_and_exit([(y_h.ap()[dc * 128:(dc + 1) * 128, :],
                                 Hb[dc][:], rHb[dc], "gpsimd")
                                for dc in range(DC)])
                return nc

            rank_ps = pp[6]
            rrkp = rpp[6]
            for tb in range(NB):
                for tb2 in range(tb + 1):
                    em.op("tensor",
                          lambda E_, t=tb, t2=tb2, NE=E: E_.matmul(
                              rank_ps[:, t * NE:(t + 1) * NE],
                              (ustrict if t2 == t else ufull)[:],
                              oh_bf[:, t2 * NE:(t2 + 1) * NE],
                              start=(t2 == 0 and t == 0),
                              stop=(t2 == t and t == NB - 1),
                              skip_group_check=True),
                          reads=([rohb, rconst] if tb2 in (0, tb) else ()),
                          writes=([rrkp] if tb2 in (0, tb) else ()))
            em.op("scalar",
                  lambda E_, NE=E: E_.copy(rank_f[:],
                                           rank_ps[:, 0:NB * NE]),
                  reads=[rrkp], writes=[rrk])
            em.op("vector",
                  lambda E: E.tensor_scalar_add(rank_f[:], rank_f[:], 1.0),
                  reads=[rrk], writes=[rrk])
            em.op("vector",
                  lambda E: E.tensor_mul(rank_f[:], rank_f[:], oh_f[:]),
                  reads=[rrk, rohf], writes=[rrk])
            em.op("vector",
                  lambda E: E.tensor_scalar_add(rank_f[:], rank_f[:], -1.0),
                  reads=[rrk], writes=[rrk])
            em.op("vector",
                  lambda E: E.tensor_copy(rank16[:], rank_f[:]),
                  reads=[rrk], writes=[rrk16])

            if debug_stop == "router":
                em.flush()
                _dump_and_exit([(y_h.ap()[0:128, 0:NB * E], rank_f[:], rrk,
                                 "sync")])
                return nc

            def load_guchunk(ppe, prouted, src, dst, rr, ic, csem):
                hf = ic // (IC // 2)
                cc = (ic % (IC // 2)) * 128
                ap = (src.ap()[ppe, hf] if prouted else src.ap()[hf])
                dma(dst[:].rearrange("p (c n) -> p c n", c=DC),
                    ap.rearrange("c p n -> p c n")[:, :, cc:cc + 128],
                    writes=[rr], eng="gpsimd", sem=csem)

            def load_wd(ppe, prouted, d_h, ic):
                src = d_h.ap()[ppe] if prouted else d_h.ap()
                dma(wd_bf[ic % 2][:], src[ic * 128:(ic + 1) * 128, :],
                    writes=[rwd[ic % 2]], eng="gpsimd", sem=f"dwd{ic % 2}")

            shsc = ExitStack()
            gres = sbuf("gres", [128, 2 * DC * IDIM], BF16, shsc)
            rgres = Res()
            for mi, src in ((0, shg_h), (1, shu_h)):
                for hf in range(2):
                    dma(gres[:].rearrange("p (m c n) -> p m c n",
                                          m=2, c=DC)[:, mi]
                        [:, :, hf * IH:(hf + 1) * IH],
                        src.ap()[hf].rearrange("c p n -> p c n"),
                        writes=[rgres], eng="gpsimd", sem=f"dgrs{mi}")
            for slab in range(NTC):
                sl = slice(slab * 512, (slab + 1) * 512)
                for ic in range(IC):
                    pg, pu = pp[(ic % 2) * 2], pp[(ic % 2) * 2 + 1]
                    rpg, rpu = rpp[(ic % 2) * 2], rpp[(ic % 2) * 2 + 1]
                    mmg([(pg[:],
                          gres[:, dc * IDIM + ic * 128:
                               dc * IDIM + (ic + 1) * 128],
                          Hb[dc][:, sl]) for dc in range(DC)],
                        writes=[rpg], reads=[rgres] + rHb)
                    mmg([(pu[:],
                          gres[:, DC * IDIM + dc * IDIM + ic * 128:
                               DC * IDIM + dc * IDIM + (ic + 1) * 128],
                          Hb[dc][:, sl]) for dc in range(DC)],
                        writes=[rpu], reads=[rgres] + rHb)
                    em.op("scalar",
                          lambda E, p=pg, i=ic: E.activation(
                              silu_sb[i % 2][:], p[:], AF.Silu),
                          reads=[rpg], writes=[rsilu[ic % 2]])
                    em.op("vector",
                          lambda E, i=ic, p=pu: E.tensor_mul(
                              act_sl[:, i * 512:(i + 1) * 512],
                              silu_sb[i % 2][:], p[:]),
                          reads=[rsilu[ic % 2], rpu], writes=[ract])
                for ic in range(IC):
                    load_wd("sh", False, shd_h, ic)
                    for dc in range(DC):
                        em.op("tensor",
                              lambda E, i=ic, d=dc, b2=ic % 2,
                              a=(ic == 0), z=(ic == IC - 1):
                              E.matmul(pp[d][:],
                                       wd_bf[b2][:, d * 128:(d + 1) * 128],
                                       act_sl[:, i * 512:(i + 1) * 512],
                                       start=a, stop=z,
                                       skip_group_check=True),
                              reads=[rwd[ic % 2], ract],
                              writes=([rpp[dc]] if ic in (0, IC - 1)
                                      else ()))
                for dc in range(DC):
                    em.op("vector",
                          lambda E, d=dc, s=sl: E.scalar_tensor_tensor(
                              Xf[d][:, s], pp[d][:],
                              mod_sb[:, GB + d:GB + d + 1], Xf[d][:, s],
                              ALU.mult, ALU.add),
                          reads=[rpp[dc], rmod, rXf[dc]], writes=[rXf[dc]])
            em.flush()
            shsc.close()
            h2tm = sbuf("h2tm", [128, NB * D], BF16, ph4); rh2t = Res()
            Pg_buf = sbuf("Pg_buf", [128, NB * CAP], BF16, ph4); rPg = Res()
            Ps_buf = sbuf("Ps_buf", [128, NSC * S], BF16, ph4); rPs = Res()
            Xg = sbuf("Xg", [128, DC * CAP], BF16, ph4); rXg = Res()
            Yg = sbuf("Yg", [128, NSC * D], BF16, ph4); rYg = Res()

            for tb in range(NB):
                for dc in range(DC):
                    pb = pp[(tb * DC + dc) % 2]
                    rpb = rpp[(tb * DC + dc) % 2]
                    em.op("tensor",
                          lambda E, t=tb, d=dc, p=pb: E.transpose(
                              p[:, 0:64].bitcast(BF16),
                              Hb[d][:, t * 128:(t + 1) * 128],
                              identb[:]),
                          reads=[rHb[dc], rconst], writes=[rpb])
                    em.op("vector",
                          lambda E, t=tb, d=dc, p=pb: E.tensor_copy(
                              h2tm[:, t * D + d * 128:t * D + (d + 1) * 128],
                              p[:, 0:64].bitcast(BF16)),
                          reads=[rpb], writes=[rh2t])

            def build_Pg(e):
                for tb in range(NB):
                    em.op("vector",
                          lambda E_, t=tb, e=e, NE=E: E_.tensor_scalar(
                              Pg_buf[:, t * CAP:t * CAP + CAP],
                              iotaf[:, 0:CAP],
                              rank_f[:, t * NE + e:t * NE + e + 1],
                              score_f[:, t:t + 1],
                              ALU.is_equal, ALU.mult),
                          reads=[rrk, rscf, rconst], writes=[rPg])

            def build_Ps_tc(e, tc):
                if True:
                    for t4 in range(4):
                        tb = tc * 4 + t4
                        em.op("tensor",
                              lambda E_, t=tb, t4=t4, e=e, NE=E: E_.matmul(
                                  pp[0][0:1, t4 * 128:(t4 + 1) * 128],
                                  rank16[:, t * NE + e:t * NE + e + 1],
                                  identh[:], start=True, stop=True,
                                  skip_group_check=True),
                              reads=[rrk16, rconst], writes=[rpp[0]])
                    em.op("scalar",
                          lambda E_: E_.copy(rank_row[:], pp[0][0:1, :]),
                          reads=[rpp[0]], writes=[rrow])
                if True:
                    em.op("tensor",
                          lambda E_: E_.matmul(
                              pp[1][:], onesh[:], rank_row[:],
                              start=True, stop=True, skip_group_check=True),
                          reads=[rrow, rconst], writes=[rpp[1]])
                    for sc in range(NSC):
                        em.op("vector",
                              lambda E_, t=tc, s2=sc: E_.tensor_scalar(
                                  Ps_buf[:, (s2 * NTC + t) * 512:
                                         (s2 * NTC + t + 1) * 512],
                                  pp[1][:], iotasc[:, s2:s2 + 1], None,
                                  ALU.is_equal),
                              reads=[rpp[1], rconst], writes=[rPs])

            def gather_unit(e_, dc):
                mmg([(pp[dc % 2][:, 0:CAP],
                      h2tm[:, t * D + dc * 128:t * D + (dc + 1) * 128],
                      Pg_buf[:, t * CAP:t * CAP + CAP])
                     for t in range(NB)],
                    writes=[rpp[dc % 2]], reads=[rh2t, rPg])
                em.op("vector",
                      lambda E_, d=dc: E_.tensor_copy(
                          Xg[:, d * CAP:(d + 1) * CAP],
                          pp[d % 2][:, 0:CAP]),
                      reads=[rpp[dc % 2]], writes=[rXg])

            build_Pg(0)
            for dc in range(DC):
                gather_unit(0, dc)
            for e in range(E):
                if e == 0:
                    load_guchunk(e, True, rtg_h, wg_ch[0], rwg[0], 0, "dwg0")
                    load_guchunk(e, True, rtu_h, wu_ch[0], rwu[0], 0, "dwu0")
                for ic in range(IC):
                    cb_ = ic % NGB
                    for pf in ((1, 2) if ic == 0 else (2,)):
                        if ic + pf < IC:
                            load_guchunk(e, True, rtg_h,
                                         wg_ch[(ic + pf) % NGB],
                                         rwg[(ic + pf) % NGB], ic + pf,
                                         f"dwg{(ic + pf) % NGB}")
                            load_guchunk(e, True, rtu_h,
                                         wu_ch[(ic + pf) % NGB],
                                         rwu[(ic + pf) % NGB], ic + pf,
                                         f"dwu{(ic + pf) % NGB}")
                    pg, pu = pp[2 + (ic % 2)], pp[4 + (ic % 2)]
                    rpg, rpu = rpp[2 + (ic % 2)], rpp[4 + (ic % 2)]
                    if ic % 4 == 3:
                        build_Ps_tc(e, ic // 4)
                    mmg([(pg[:, 0:CAP],
                          wg_ch[cb_][:, dc * 128:(dc + 1) * 128],
                          Xg[:, dc * CAP:(dc + 1) * CAP])
                         for dc in range(DC)],
                        writes=[rpg], reads=[rwg[cb_], rXg])
                    mmg([(pu[:, 0:CAP],
                          wu_ch[cb_][:, dc * 128:(dc + 1) * 128],
                          Xg[:, dc * CAP:(dc + 1) * CAP])
                         for dc in range(DC)],
                        writes=[rpu], reads=[rwu[cb_], rXg])
                    em.op("scalar",
                          lambda E, p=pg, i=ic: E.activation(
                              silu_sb[i % 2][:, 0:CAP], p[:, 0:CAP],
                              AF.Silu),
                          reads=[rpg], writes=[rsilu[ic % 2]])
                    em.op("vector",
                          lambda E, i=ic, p=pu: E.tensor_mul(
                              act_sl[:, i * 512:i * 512 + CAP],
                              silu_sb[i % 2][:, 0:CAP], p[:, 0:CAP]),
                          reads=[rsilu[ic % 2], rpu], writes=[ract])
                if e + 1 < E:
                    build_Pg(e + 1)
                gu = list(range(DC)) if e + 1 < E else []
                gui = 0
                for ic in range(IC):
                    load_wd(e, True, rtd_h, ic)
                    for sc in range(NSC):
                        for dh in range(2):
                            k = sc * 2 + dh
                            em.op("tensor",
                                  lambda E, i=ic, s2=sc, d2=dh, b2=ic % 2,
                                  k=k, a=(ic == 0), z=(ic == IC - 1):
                                  E.matmul(
                                      pp[2 + k][:],
                                      act_sl[:, i * 512 + s2 * 128:
                                             i * 512 + (s2 + 1) * 128],
                                      wd_bf[b2][:, d2 * 512:(d2 + 1) * 512],
                                      start=a, stop=z,
                                      skip_group_check=True),
                                  reads=[rwd[ic % 2], ract],
                                  writes=([rpp[2 + k]] if ic in (0, IC - 1)
                                          else ()))
                    while gui < len(gu) * (ic + 1) // IC:
                        gather_unit(e + 1, gu[gui])
                        gui += 1
                    if ic == IC - 4 and e + 1 < E:
                        load_guchunk(e + 1, True, rtg_h, wg_ch[0], rwg[0],
                                     0, "dwg0")
                        load_guchunk(e + 1, True, rtu_h, wu_ch[0], rwu[0],
                                     0, "dwu0")
                for k in range(NSC * 2):
                    em.op("vector",
                          lambda E_, k=k: E_.tensor_copy(
                              Yg[:, k * 512:(k + 1) * 512], pp[2 + k][:]),
                          reads=[rpp[2 + k]], writes=[rYg])
                for dc in range(DC):
                    for tc in range(NTC):
                        k = 6 + (dc * NTC + tc) % 2
                        chain = [(pp[k][:],
                                  Yg[:, sc * D + dc * 128:
                                     sc * D + (dc + 1) * 128],
                                  Ps_buf[:, (sc * NTC + tc) * 512:
                                         (sc * NTC + tc + 1) * 512])
                                 for sc in range(NSC)]
                        mmg(chain, writes=[rpp[k]], reads=[rYg, rPs])
                        em.op("vector",
                              lambda E_, d=dc, t=tc, k=k:
                              E_.scalar_tensor_tensor(
                                  Xf[d][:, t * 512:(t + 1) * 512],
                                  pp[k][:],
                                  mod_sb[:, GB + d:GB + d + 1],
                                  Xf[d][:, t * 512:(t + 1) * 512],
                                  ALU.mult, ALU.add),
                              reads=[rpp[k], rmod, rXf[dc]],
                              writes=[rXf[dc]])
            em.flush()

        rout = Res()
        evs = []
        for dc in range(DC):
            evs.append(dma(y_h.ap()[dc * 128:(dc + 1) * 128, :], Xf[dc][:],
                           writes=[rout], reads=[rXf[dc]], sem="so"))
        for ev in evs:
            em.wait_event("sync", ev)
        em.flush()
        em.finalize()

    return nc


def _pack(inputs, S, NH, IDIM, E):
    f = np.ascontiguousarray
    FC = NH // 2
    IH = IDIM // 2
    wq = inputs["L0_attn_W_Q"].transpose(0, 2, 1).reshape(NH, DC, 128, 64)
    wk = inputs["L0_attn_W_K"].transpose(0, 2, 1).reshape(NH, DC, 128, 64)
    wv = inputs["L0_attn_W_V"].transpose(0, 2, 1).reshape(NH, DC, 128, 64)

    def pair(w):
        return f(w.reshape(FC, 2, DC, 128, 64).transpose(0, 2, 3, 1, 4)
                 .reshape(FC, DC, 128, 128))

    wo = f(np.asarray(inputs["L0_attn_W_O"]).T.reshape(FC, 128, DC, 128)
           .transpose(0, 2, 1, 3).reshape(FC * DC, 128, 128))
    gw = f(np.asarray(inputs["L0_ffn_gate_w"]).reshape(DC, 128, E))

    def halves(w):
        return f(np.asarray(w).reshape(DC, 128, 2, IH).transpose(2, 0, 1, 3))

    out = {
        "wmod": f(np.asarray(inputs["L0_wmod"])),
        "wq2": pair(wq), "wk2": pair(wk), "wv2": pair(wv), "wo": wo,
        "gw": gw,
        "shg": halves(inputs["L0_ffn_sh_gate"]),
        "shu": halves(inputs["L0_ffn_sh_up"]),
        "shd": f(np.asarray(inputs["L0_ffn_sh_down"])),
        "rtg": f(np.stack([halves(inputs["L0_ffn_rt_gate"][e])
                           for e in range(E)])),
        "rtu": f(np.stack([halves(inputs["L0_ffn_rt_up"][e])
                           for e in range(E)])),
        "rtd": f(np.asarray(inputs["L0_ffn_rt_down"])),
    }
    i = np.arange(128)
    out["diagm"] = np.where(i[:, None] > i[None, :], NEG, 0.0).astype(
        np.float32)
    out["ones128"] = np.ones((1, 128), np.float32)
    out["nones128"] = -np.ones((1, 128), np.float32)
    out["onescol"] = np.ones((128, 1), ml_dtypes.bfloat16)
    eab = np.zeros((1, 256), ml_dtypes.bfloat16)
    eab[0, 0:64] = 1
    eab[0, 192:256] = 1
    out["eab"] = eab
    out["ones128b"] = np.ones((1, 128), ml_dtypes.bfloat16)
    out["identb"] = np.eye(128, dtype=ml_dtypes.bfloat16)
    out["identf"] = np.eye(128, dtype=np.float32)
    out["w8"] = np.broadcast_to(
        (2.0 ** np.arange(E + 1, 1, -1)).astype(np.float32), (128, E)).copy()
    out["ufull"] = np.ones((128, 128), ml_dtypes.bfloat16)
    out["ustrict"] = np.triu(np.ones((128, 128)), k=1).astype(
        ml_dtypes.bfloat16)
    out["iotaf"] = np.broadcast_to(np.arange(512, dtype=np.float32),
                                   (128, 512)).copy()
    CAP = 384
    out["iotasc"] = (np.arange(128, dtype=np.float32)[:, None] +
                     128.0 * np.arange(CAP // 128, dtype=np.float32)[None, :]
                     ).copy()
    out["identh"] = np.eye(128, dtype=np.float16)
    out["onesh"] = np.ones((1, 128), np.float16)
    e65 = np.zeros((68, 256), ml_dtypes.bfloat16)
    e65[64:68, 0:64] = 1
    e65[64:68, 192:256] = 1
    out["e65"] = e65
    return out


def kernel(**inputs):
    S, NH, IDIM, E = 2048, 16, 2048, 8
    B = np.asarray(inputs["L0_attn_x"]).shape[0]
    shared = _pack(inputs, S, NH, IDIM, E)
    x = np.asarray(inputs["L0_attn_x"])
    cv = np.asarray(inputs["L0_c"])
    in_maps = []
    for c in range(B):
        m = dict(shared)
        m["x"] = np.ascontiguousarray(x[c].T)
        m["cvec"] = np.ascontiguousarray(cv[c].reshape(DC, 128).T)
        in_maps.append(m)
    nc = build_nc(S, NH, IDIM, E)
    resu = run_bass_kernel_spmd(nc, in_maps, core_ids=list(range(B)))
    out = np.stack([r["y"].T for r in resu.results])
    return np.ascontiguousarray(out.astype(np.float32))
